# version 36
# baseline (speedup 1.0000x reference)
import math
from collections import deque
from contextlib import ExitStack

import numpy as np
import concourse.bass as bass
import concourse.mybir as mybir
from concourse.bass_utils import run_bass_kernel_spmd

F32 = mybir.dt.float32
BF16 = mybir.dt.bfloat16
I32 = mybir.dt.int32
AF = mybir.ActivationFunctionType
ALU = mybir.AluOpType

T, D, H, DH, HLF, FF = 4096, 1024, 8, 128, 64, 3072
NB = T // 128
NT = T // 512
NHC = FF // 128
NDC = D // 128

LN_EPS = 1e-5
RMS_EPS_128 = 128.0 * 1e-6
INV127 = 1.0 / 127.0
NEG_BIG = -1.0e30

TWO_PI = 2.0 * math.pi
TPI_HI = float(np.float32(TWO_PI))
TPI_LO = float(np.float64(TWO_PI) - np.float64(np.float32(TWO_PI)))
PI_2 = math.pi / 2.0
P2_HI = float(np.float32(PI_2))
P2_LO = float(np.float64(PI_2) - np.float64(np.float32(PI_2)))
INV2PI = 1.0 / TWO_PI
PI_CLAMP = float(np.nextafter(np.float32(math.pi), np.float32(0.0)))

ENGINES = ("tensor", "vector", "scalar", "sync", "gpsimd")


class Sched:

    def __init__(self, nc):
        self.nc = nc
        self.items = {e: [] for e in ENGINES}
        self._sems = {}
        self.cnt = {}

    def sem(self, name):
        if name not in self._sems:
            self._sems[name] = self.nc.alloc_semaphore(name=name)
            self.cnt[name] = 0
        return self._sems[name]

    def emit(self, eng, fn, waits=(), inc=None, inc_by=1):
        if SERIAL:
            waits = list(waits) + [(s, v) for s, v in self.cnt.items()]
        waits = [(s, v) for (s, v) in waits if v > 0]
        for s, v in waits:
            assert s in self.cnt and v <= self.cnt[s], (
                f"wait on future value: {s} {v} > {self.cnt.get(s)}")
        if inc is not None:
            self.sem(inc)
            self.cnt[inc] += inc_by
        self.items[eng].append((waits, fn, inc, inc_by))
        return self.cnt[inc] if inc is not None else None

    def flush(self):
        nc = self.nc
        active = [e for e in ENGINES if self.items[e]]
        if not active:
            return
        own = {"tensor": "TE", "vector": "VE", "scalar": "SE", "gpsimd": "GE"}
        with nc.Block() as block:
            for e in active:
                items = self.items[e]

                def body(eng, items=items, e=e):
                    waited = {}
                    own_cnt = 0
                    for waits, fn, inc, inc_by in items:
                        for s, v in waits:
                            if s == own.get(e) and v <= own_cnt:
                                continue
                            if waited.get(s, -1) >= v:
                                continue
                            waited[s] = v
                            eng.wait_ge(self._sems[s], v)
                        inst = fn(eng)
                        if inc is not None:
                            assert inst is not None
                            inst.then_inc(self._sems[inc], inc_by)
                            if inc == own.get(e):
                                own_cnt += inc_by

                getattr(block, e)(body)
        for e in ENGINES:
            self.items[e] = []


class Pool:

    def __init__(self, resources):
        self.q = deque((r, []) for r in resources)

    def acquire(self):
        r, rel = self.q.popleft()
        return r, list(rel)

    def release(self, r, rel):
        if isinstance(rel, tuple):
            rel = [rel]
        self.q.append((r, rel or []))


def bcast_free(ap, reps, where=1):
    dims = [list(d) for d in ap.ap]
    dims.insert(where, [0, reps])
    return bass.AP(ap.tensor, ap.offset, dims)


def flat2(ap3):
    return ap3.rearrange("p a b -> p (a b)")


DEBUG_PHASES = (1, 2, 3)
SERIAL = False


def build_program(dbg=None):
    nc = bass.Bass("TRN2", target_bir_lowering=False,
                   detect_race_conditions=False)
    dbg = dbg or []

    x_h = nc.dram_tensor("x", [T, D], F32, kind="ExternalInput").ap()
    wq_h = nc.dram_tensor("wq", [D, D], F32, kind="ExternalInput").ap()
    wk_h = nc.dram_tensor("wk", [D, D], F32, kind="ExternalInput").ap()
    wv_h = nc.dram_tensor("wv", [D, D], F32, kind="ExternalInput").ap()
    wo_h = nc.dram_tensor("wo", [D, D], F32, kind="ExternalInput").ap()
    wg_h = nc.dram_tensor("wg", [FF, D], I32, kind="ExternalInput").ap()
    wu_h = nc.dram_tensor("wu", [FF, D], I32, kind="ExternalInput").ap()
    wd_h = nc.dram_tensor("wd", [D, FF], I32, kind="ExternalInput").ap()
    out_h = nc.dram_tensor("out", [T, D], F32, kind="ExternalOutput").ap()

    xn_d = nc.dram_tensor("xn_d", [T, D], BF16, kind="Internal").ap()
    qr_d = nc.dram_tensor("qr_d", [T, D], BF16, kind="Internal").ap()
    kr_d = nc.dram_tensor("kr_d", [T, D], BF16, kind="Internal").ap()
    wg_b = nc.dram_tensor("wg_b", [FF, D], BF16, kind="Internal").ap()
    wu_b = nc.dram_tensor("wu_b", [FF, D], BF16, kind="Internal").ap()
    wd_b = nc.dram_tensor("wd_b", [D, FF], BF16, kind="Internal").ap()

    dbg_h = {}
    for name, shape, dt in dbg:
        dbg_h[name] = nc.dram_tensor(name, shape, dt, kind="ExternalOutput").ap()

    import ml_dtypes
    pos_np = np.arange(T, dtype=np.float32).reshape(1, T)
    dfrac_np = (-(np.arange(HLF, dtype=np.float64)) / HLF).astype(np.float32).reshape(1, HLF)
    ten4_np = np.full((1, 1), 10000.0, dtype=np.float32)
    jj = np.arange(128).reshape(128, 1)
    xx = np.arange(896).reshape(1, 896)
    maskb_np = np.where(jj > xx - 384, NEG_BIG, 0.0).astype(ml_dtypes.bfloat16)
    onesc_np = np.ones((128, 1), dtype=ml_dtypes.bfloat16)
    onesr_np = np.ones((1, 128), dtype=np.float32)
    onesrb_np = np.ones((1, 128), dtype=ml_dtypes.bfloat16)
    oneh_np = np.zeros((128, 136), dtype=ml_dtypes.bfloat16)
    oneh_np[:, 7] = 1
    onehb_np = np.zeros((8, 1024), dtype=ml_dtypes.bfloat16)
    for _h in range(8):
        onehb_np[_h, _h * 128:(_h + 1) * 128] = 1
    id_np = np.eye(128, dtype=ml_dtypes.bfloat16)

    pos_c = nc.inline_tensor(pos_np, name="pos_c").ap()
    dfrac_c = nc.inline_tensor(dfrac_np, name="dfrac_c").ap()
    ten4_c = nc.inline_tensor(ten4_np, name="ten4_c").ap()
    maskb_c = nc.inline_tensor(maskb_np, name="maskb_c").ap()
    onesc_c = nc.inline_tensor(onesc_np, name="onesc_c").ap()
    onesr_c = nc.inline_tensor(onesr_np, name="onesr_c").ap()
    onesrb_c = nc.inline_tensor(onesrb_np, name="onesrb_c").ap()
    oneh_c = nc.inline_tensor(oneh_np, name="oneh_c").ap()
    onehb_c = nc.inline_tensor(onehb_np, name="onehb_c").ap()
    id_c = nc.inline_tensor(id_np, name="id_c").ap()

    S = Sched(nc)

    def dma(queue, dst, src, waits=(), inc=None):
        return S.emit(queue, lambda e, d=dst, s=src: e.dma_start(d, s),
                      waits=waits, inc=inc, inc_by=16)

    def dmat(dst, src, waits=(), inc=None):
        return S.emit("sync", lambda e, d=dst, s=src: e.dma_start_transpose(d, s),
                      waits=waits, inc=inc, inc_by=16)

    with ExitStack() as top:
        pb = [top.enter_context(nc.psum_tensor(f"pb{i}", [128, 512], F32))
              for i in range(8)]
        psum = Pool(list(range(8)))

        with ExitStack() as s12:
            v_sb = s12.enter_context(nc.sbuf_tensor("v_sb", [128, NB, D], BF16))

            with ExitStack() as s1:
                def A(name, shape, dt):
                    return s1.enter_context(nc.sbuf_tensor(name, shape, dt))

                wq_sb = A("wq_sb", [128, NDC, D], BF16)
                wk_sb = A("wk_sb", [128, NDC, D], BF16)
                wv_sb = A("wv_sb", [128, NDC, D], BF16)
                x_bufs = [A(f"x{i}", [128, D], F32) for i in range(2)]
                xn_bufs = [A(f"xn{i}", [128, D], BF16) for i in range(2)]
                xnT_bufs = [A(f"xnT{i}", [128, NDC, 512], BF16) for i in range(2)]
                q_sb4 = A("q_sb4", [128, 4, D], BF16)
                k_sb4 = A("k_sb4", [128, 4, D], BF16)
                qr_pair = A("qr_pair", [128, 2, D], BF16)
                kr_pair = A("kr_pair", [128, 2, D], BF16)
                cos_sb = A("cos_sb", [128, NB, HLF], BF16)
                sin_sb = A("sin_sb", [128, NB, HLF], BF16)
                sqd = A("sqd", [128, D], F32)
                sqv = A("sqv", [128, D], BF16)
                ssx = A("ssx", [128, 4], F32)
                sux = A("sux", [128, 4], F32)
                mean_t = A("mean_t", [128, 4], F32)
                var_t = A("var_t", [128, 4], F32)
                tmp_t = A("tmp_t", [128, 4], F32)
                std_t = A("std_t", [128, 4], F32)
                rstd_t = A("rstd_t", [128, 4], F32)
                nmr_t = A("nmr_t", [128, 4], F32)
                ssq_t = A("ssq_t", [128, 4, H], F32)
                ssk_t = A("ssk_t", [128, 4, H], F32)
                rsq_t = A("rsq_t", [128, 8, H], F32)
                rsk_t = A("rsk_t", [128, 8, H], F32)
                rc_t = A("rc_t", [128, 2, H, HLF], BF16)
                rs_t = A("rs_t", [128, 2, H, HLF], BF16)
                rope_u = A("rope_u", [128, 2, H, HLF], BF16)
                rope_w = A("rope_w", [128, 2, H, HLF], BF16)
                pos_sb = A("pos_sb", [1, 2048], F32)
                dfrac_sb = A("dfrac_sb", [1, HLF], F32)
                ten4_sb = A("ten4_sb", [1, 1], F32)
                ln4_sb = A("ln4_sb", [1, 1], F32)
                efrq_sb = A("efrq_sb", [1, HLF], F32)
                frq_sb = A("frq_sb", [1, HLF], F32)
                tg_a = A("tg_a", [128, 512], F32)
                tg_r = A("tg_r", [128, 512], F32)
                tg_ki = A("tg_ki", [128, 512], I32)
                id_sb = A("id_sb", [128, 128], BF16)

                conv_chunks = []
                for (src, dst, rows) in ((wg_h, wg_b, FF), (wu_h, wu_b, FF),
                                         (wd_h, wd_b, D)):
                    step = 256 if rows == FF else 128
                    for r0 in range(0, rows, step):
                        conv_chunks.append((src, dst, r0, step))
                conv_total = [0]

                def emit_conv_chunk(queue="gpsimd", waits=None):
                    if conv_chunks:
                        src, dst, r0, step = conv_chunks.pop(0)
                        conv_total[0] = dma(queue, dst[r0:r0 + step, :],
                                            src[r0:r0 + step, :],
                                            waits=waits or [], inc="CONV")

                dma("sync", dfrac_sb[:, :], dfrac_c, inc="CST")
                pos_ld_n = {0: dma("sync", pos_sb[:, :],
                                    pos_c[0:1, 0:2048], inc="POS")}
                idl_n = dma("sync", id_sb[:, :], id_c, inc="CST")
                cst_n = dma("sync", ten4_sb[:, :], ten4_c, inc="CST")
                wqld = dma("gpsimd", wq_sb[:, :, :],
                           wq_h.rearrange("(c p) m -> p c m", p=128), inc="WQL")
                wkld = dma("gpsimd", wk_sb[:, :, :],
                           wk_h.rearrange("(c p) m -> p c m", p=128), inc="WKL")
                wvld = dma("gpsimd", wv_sb[:, :, :],
                           wv_h.rearrange("(c p) m -> p c m", p=128), inc="WVL")

                se_n = S.emit("scalar", lambda e: e.activation(
                    ln4_sb[:, :], ten4_sb[:, :], AF.Ln), waits=[("CST", cst_n)],
                    inc="SE")
                ve_n = S.emit("vector", lambda e: e.tensor_scalar_mul(
                    efrq_sb[:, :], dfrac_sb[:, :], ln4_sb[0:1, 0:1]),
                    waits=[("SE", se_n)], inc="VE")
                frq_ready = S.emit("scalar", lambda e: e.activation(
                    frq_sb[:, :], efrq_sb[:, :], AF.Exp), waits=[("VE", ve_n)],
                    inc="SE")

                last_tab_se = [0]

                def range_reduce(dst2d, rad_ap, extra_pi2, first_waits):
                    fw = list(first_waits) + [("SE", last_tab_se[0])]
                    if extra_pi2:
                        S.emit("vector", lambda e: e.tensor_scalar(
                            tg_r[:, :], rad_ap, P2_HI, P2_LO, ALU.add, ALU.add),
                            waits=fw)
                    else:
                        S.emit("vector", lambda e: e.tensor_copy(
                            tg_r[:, :], rad_ap), waits=fw)
                    rad_freed = S.emit("vector", lambda e: e.engine_nop(), inc="VE")
                    for _ in range(2):
                        S.emit("vector", lambda e: e.tensor_scalar(
                            tg_a[:, :], tg_r[:, :], INV2PI, 0.5,
                            ALU.mult, ALU.add))
                        S.emit("vector", lambda e: e.tensor_copy(
                            tg_ki[:, :], tg_a[:, :]))
                        S.emit("vector", lambda e: e.tensor_copy(
                            tg_a[:, :], tg_ki[:, :]))
                        S.emit("vector", lambda e: e.scalar_tensor_tensor(
                            tg_r[:, :], tg_a[:, :], -TPI_HI, tg_r[:, :],
                            ALU.mult, ALU.add))
                        S.emit("vector", lambda e: e.scalar_tensor_tensor(
                            tg_r[:, :], tg_a[:, :], -TPI_LO, tg_r[:, :],
                            ALU.mult, ALU.add))
                    S.emit("vector", lambda e: e.tensor_scalar_min(
                        tg_r[:, :], tg_r[:, :], PI_CLAMP))
                    v_n = S.emit("vector", lambda e: e.tensor_scalar_max(
                        tg_r[:, :], tg_r[:, :], -PI_CLAMP), inc="VE")
                    s_n = S.emit("scalar", lambda e, d=dst2d: e.activation(
                        d, tg_r[:, :], AF.Sin), waits=[("VE", v_n)], inc="SE")
                    last_tab_se[0] = s_n
                    return s_n, rad_freed

                table_se = [None] * 4
                rad_te = [0] * 4

                def emit_table_chunk(c):
                    bank, rel = psum.acquire()
                    te_n = None
                    for b in range(8):
                        w = ([("POS", pos_ld_n[c // 2]), ("SE", frq_ready)] +
                             (rel if b == 0 else []))
                        te_n = S.emit("tensor", lambda e, bk=bank, c=c, b=b:
                                      e.matmul(pb[bk][:, b * 64:(b + 1) * 64],
                                               pos_sb[0:1, (c % 2) * 1024 + b * 128:
                                                      (c % 2) * 1024 + (b + 1) * 128],
                                               frq_sb[0:1, :], start=True, stop=True),
                                      waits=w, inc="TE")
                    rad_te[c] = te_n
                    range_reduce(flat2(sin_sb[:, 8 * c:8 * c + 8, :]),
                                 pb[bank][:, :], False, [("TE", te_n)])
                    table_se[c], rad_freed = range_reduce(
                        flat2(cos_sb[:, 8 * c:8 * c + 8, :]), pb[bank][:, :],
                        True, [])
                    psum.release(bank, ("VE", rad_freed))

                x_pool = Pool([0, 1])
                xn_pool = Pool([0, 1])
                xnT_pool = Pool([0, 1])
                qk_pool = Pool([0, 1, 2, 3])
                qr_pool = Pool([0, 1])
                xnd_cnt = [0, 0]
                qrd_cnt = [0, 0]
                krd_cnt = [0, 0]
                last_rr = {}
                xnt_ready = {}

                emit_table_chunk(0)

                rope_defer = []
                qr_mark = {}
                kr_mark = {}
                xnt_first_rel = {}
                last_apply = {}
                qr_store = [0]
                kr_store = [0]

                def emit_ropes():
                    work, rope_defer[:] = list(rope_defer), []
                    if not work:
                        return
                    for a0, a1 in zip(work[0::2], work[1::2]):
                        blk = a0["blk"]
                        par8 = a0["par8"]
                        tchunk = blk // 8
                        for proj, sb4, rsb, dstp, dsem, dcnt, dd, evp, rrp in (
                                ("q", q_sb4, rsq_t, qr_pair, "QRS", qr_store,
                                 qr_d, max(a0["evq"], a1["evq"]),
                                 max(a0["rrq"], a1["rrq"])),
                                ("k", k_sb4, rsk_t, kr_pair, "KRS", kr_store,
                                 kr_d, max(a0["evk"], a1["evk"]),
                                 max(a0["rrk"], a1["rrk"]))):
                            b0 = blk % 4
                            s4 = sb4[:, b0:b0 + 2, :].rearrange(
                                "p b (h d) -> p b h d", d=DH)
                            d4 = dstp[:, :, :].rearrange(
                                "p b (h d) -> p b h d", d=DH)
                            rstd_b = bcast_free(rsb[:, par8:par8 + 2, :], HLF,
                                                where=3)
                            cosb = bcast_free(cos_sb[:, blk:blk + 2, :], H, where=2)
                            sinb = bcast_free(sin_sb[:, blk:blk + 2, :], H, where=2)
                            w0 = [("SE", evp), ("SE", table_se[tchunk]),
                                  ("VE", rrp), (dsem, dcnt[0])]
                            S.emit("vector", lambda e, rb=rstd_b, cb=cosb:
                                   e.tensor_mul(rc_t[:, :, :, :], rb, cb), waits=w0)
                            S.emit("vector", lambda e, rb=rstd_b, sb2=sinb:
                                   e.tensor_mul(rs_t[:, :, :, :], rb, sb2))
                            S.emit("vector", lambda e, s4=s4: e.tensor_mul(
                                rope_u[:, :, :, :], s4[:, :, :, 0:HLF],
                                rc_t[:, :, :, :]))
                            S.emit("vector", lambda e, s4=s4: e.tensor_mul(
                                rope_w[:, :, :, :], s4[:, :, :, HLF:DH],
                                rs_t[:, :, :, :]))
                            S.emit("vector", lambda e, d4=d4: e.tensor_sub(
                                d4[:, :, :, 0:HLF], rope_u[:, :, :, :],
                                rope_w[:, :, :, :]))
                            S.emit("vector", lambda e, s4=s4: e.tensor_mul(
                                rope_u[:, :, :, :], s4[:, :, :, 0:HLF],
                                rs_t[:, :, :, :]))
                            S.emit("vector", lambda e, s4=s4: e.tensor_mul(
                                rope_w[:, :, :, :], s4[:, :, :, HLF:DH],
                                rc_t[:, :, :, :]))
                            ve_n = S.emit("vector", lambda e, d4=d4: e.tensor_add(
                                d4[:, :, :, HLF:DH], rope_u[:, :, :, :],
                                rope_w[:, :, :, :]), inc="VE")
                            dst_h = dd[blk * 128:(blk + 2) * 128, :].rearrange(
                                "(b p) m -> p b m", p=128)
                            dcnt[0] = dma("gpsimd", dst_h, dstp[:, :, :],
                                          waits=[("VE", ve_n)], inc=dsem)
                            if proj == "k":
                                rel = ("VE", ve_n)
                                qk_pool.release(a0["qks"], rel)
                                qk_pool.release(a1["qks"], rel)
                    t_done = work[-1]["blk"] // 4
                    qr_mark[t_done] = qr_store[0]
                    kr_mark[t_done] = kr_store[0]

                ln_store_mark = {}
                ln_apply_mark = {}
                tr_mark = {}
                xnt_slot_of = {}
                xnt_blk_n = {}

                def emit_ln_block(t, b):
                        if b == 0:
                            xnts0, _xr = xnT_pool.acquire()
                            xnt_slot_of[t] = xnts0
                            if _xr:
                                xnt_first_rel[t] = _xr
                        if True:
                            blk = 4 * t + b
                            par = blk % 4
                            xslot, xrel = x_pool.acquire()
                            xn_ld = dma("sync", x_bufs[xslot][:, :],
                                        x_h[blk * 128:(blk + 1) * 128, :],
                                        waits=xrel, inc=f"XL{xslot}")
                            xb = x_bufs[xslot]
                            S.emit("scalar", lambda e, xb=xb, par=par: e.activation(
                                sqd[:, :], xb[:, :], AF.Square,
                                accum_out=ssx[:, par:par + 1]),
                                waits=[(f"XL{xslot}", xn_ld)])
                            se_n = S.emit("scalar", lambda e, xb=xb, par=par: e.activation(
                                sqd[:, :], xb[:, :], AF.Identity,
                                accum_out=sux[:, par:par + 1]), inc="SE")
                            S.emit("vector", lambda e, par=par: e.tensor_scalar_mul(
                                mean_t[:, par:par + 1], sux[:, par:par + 1], 1.0 / D),
                                waits=[("SE", se_n)])
                            S.emit("vector", lambda e, par=par: e.tensor_scalar_mul(
                                var_t[:, par:par + 1], ssx[:, par:par + 1], 1.0 / D))
                            S.emit("vector", lambda e, par=par: e.scalar_tensor_tensor(
                                tmp_t[:, par:par + 1], mean_t[:, par:par + 1], -1.0,
                                mean_t[:, par:par + 1], ALU.mult, ALU.mult))
                            S.emit("vector", lambda e, par=par: e.tensor_add(
                                var_t[:, par:par + 1], var_t[:, par:par + 1],
                                tmp_t[:, par:par + 1]))
                            ve_n = S.emit("vector", lambda e, par=par: e.tensor_scalar_add(
                                var_t[:, par:par + 1], var_t[:, par:par + 1], LN_EPS),
                                inc="VE")
                            se_n = S.emit("scalar", lambda e, par=par: e.activation(
                                std_t[:, par:par + 1], var_t[:, par:par + 1], AF.Sqrt),
                                waits=[("VE", ve_n)], inc="SE")
                            S.emit("vector", lambda e, par=par: e.reciprocal(
                                rstd_t[:, par:par + 1], std_t[:, par:par + 1]),
                                waits=[("SE", se_n)])
                            xnslot, xnrel = xn_pool.acquire()
                            w = [(f"XNS{xnslot}", xnd_cnt[xnslot])] + xnrel
                            ap_e, ap_n0 = "VE", S.emit(
                                "vector", lambda e, xb=xb, par=par, sl=xnslot:
                                e.tensor_scalar(
                                    xn_bufs[sl][:, :], xb[:, :],
                                    mean_t[:, par:par + 1],
                                    rstd_t[:, par:par + 1],
                                    ALU.subtract, ALU.mult),
                                waits=w, inc="VE")
                            ap_n = ap_n0
                            x_pool.release(xslot, ("VE", ap_n))
                            last_apply[par] = 0
                            xnd_cnt[xnslot] = dma(
                                "gpsimd", xn_d[blk * 128:(blk + 1) * 128, :],
                                xn_bufs[xnslot][:, :],
                                waits=[("VE", ap_n)], inc=f"XNS{xnslot}")
                            ln_apply_mark[(t, b)] = (xnslot, ap_n,
                                                     xnd_cnt[xnslot])

                        ln_store_mark[t] = (xnd_cnt[0], xnd_cnt[1])

                def emit_tr_block(t, b):
                    xnts = xnt_slot_of[t]
                    xnslot, ap_n, xnd_n = ln_apply_mark.pop((t, b))
                    se_last = None
                    for g in range(2):
                        bank, rel = psum.acquire()
                        te_g = None
                        for j in range(4):
                            dc = 4 * g + j
                            w = []
                            if j == 0:
                                w = [("VE", ap_n), ("CST", idl_n)] + rel
                            te_g = S.emit(
                                "tensor",
                                lambda e, bank=bank, j=j, dc=dc, sl=xnslot:
                                e.matmul(pb[bank][:, j * 128:(j + 1) * 128],
                                         xn_bufs[sl][:, dc * 128:(dc + 1) * 128],
                                         id_sb[:, :], start=True, stop=True),
                                waits=w, inc="TE" if j == 3 else None)
                        w2 = [("TE", te_g)]
                        if g == 0:
                            w2 += xnt_first_rel.pop(t, [])
                        se_last = S.emit(
                            "scalar",
                            lambda e, bank=bank, g=g, b=b, xnts=xnts:
                            e.copy(xnT_bufs[xnts][:, 4 * g:4 * g + 4,
                                                  b * 128:(b + 1) * 128],
                                   pb[bank][:, :].rearrange(
                                       "p (c k) -> p c k", k=128)),
                            waits=w2, inc="SE")
                        psum.release(bank, ("SE", se_last))
                    xn_pool.release(xnslot, [(f"XNS{xnslot}", xnd_n),
                                             ("TE", S.cnt["TE"])])
                    tr_mark[(t, b)] = se_last

                
                for b in range(4):
                    emit_ln_block(0, b)
                    if b >= 1:
                        emit_tr_block(0, b - 1)
                emit_tr_block(0, 3)

                for t in range(NT):
                    emit_ropes()

                    xnts = xnt_slot_of[t]
                    xnT = xnT_bufs[xnts]

                    for b in range(4):
                        blk = 4 * t + b
                        par = blk % 4
                        stops = {}
                        banks = {}
                        for proj, wsb, wldnm, wldn in (
                                ("q", wq_sb, "WQL", wqld),
                                ("k", wk_sb, "WKL", wkld),
                                ("v", wv_sb, "WVL", wvld)):
                            pair = []
                            relw = []
                            for half in range(2):
                                bank, rel = psum.acquire()
                                pair.append(bank)
                                relw += rel
                            te_n = None
                            for dc in range(NDC):
                                for half in range(2):
                                    w = []
                                    if dc == 0 and half == 0:
                                        w = [("SE", tr_mark[(t, b)]),
                                             (wldnm, wldn)] + relw
                                    te_n = S.emit(
                                        "tensor",
                                        lambda e, pk=pair[half], dc=dc, b=b,
                                        wsb=wsb, half=half, xnT=xnT: e.matmul(
                                            pb[pk][:, :],
                                            xnT[:, dc, b * 128:(b + 1) * 128],
                                            wsb[:, dc, half * 512:(half + 1) * 512],
                                            start=(dc == 0), stop=(dc == NDC - 1)),
                                        waits=w,
                                        inc="TE" if (dc == NDC - 1 and half == 1)
                                        else None)
                            stops[proj] = te_n
                            banks[proj] = pair

                        qks, qkrel = qk_pool.acquire()
                        qsb, ksb = q_sb4[:, qks, :], k_sb4[:, qks, :]
                        ev = {}
                        for proj, dst in (("q", qsb), ("k", ksb), ("v", None)):
                            pair = banks[proj]
                            se_n = None
                            for half in range(2):
                                if proj == "v":
                                    d = v_sb[:, blk, half * 512:(half + 1) * 512]
                                else:
                                    d = dst[:, half * 512:(half + 1) * 512]
                                w = [("TE", stops[proj])]
                                if proj == "q" and half == 0:
                                    w += qkrel
                                se_n = S.emit("scalar",
                                              lambda e, d=d, pk=pair[half]:
                                              e.copy(d, pb[pk][:, :]),
                                              waits=w, inc="SE")
                            ev[proj] = se_n

                        if "d_q" in dbg_h:
                            dma("gpsimd", dbg_h["d_q"][blk * 128:(blk + 1) * 128, :],
                                qsb[:, :], waits=[("SE", ev["q"])], inc="DBG")
                            dma("gpsimd", dbg_h["d_k"][blk * 128:(blk + 1) * 128, :],
                                ksb[:, :], waits=[("SE", ev["k"])], inc="DBG")
                        for proj in ("q", "k", "v"):
                            pair = banks[proj]
                            relsem = ("SE", ev[proj])
                            psum.release(pair[0], relsem)
                            psum.release(pair[1], relsem)
                        for proj, dst, sb0 in (("q", ssq_t, qsb), ("k", ssk_t, ksb)):
                            w = [("VE", last_rr.get((proj, par), 0)),
                                 ("SE", ev[proj])]
                            S.emit("vector", lambda e, sb0=sb0:
                                   e.tensor_mul(sqv[:, :], sb0, sb0), waits=w)
                            ve_n = S.emit("vector", lambda e, dst=dst, par=par:
                                          e.reduce_sum(
                                              dst[:, par, :],
                                              sqv[:, :].rearrange(
                                                  "p (h d) -> p h d", d=DH),
                                              axis=mybir.AxisListType.X),
                                          inc="VE")
                            ev[proj + "sq"] = ("VE", ve_n)

                        rr = {}
                        par8 = blk % 8
                        for proj, ss, rs in (("q", ssq_t, rsq_t), ("k", ssk_t, rsk_t)):
                            ve_n = S.emit("vector", lambda e, ss=ss, par=par:
                                          e.tensor_scalar_add(
                                              ss[:, par, :], ss[:, par, :],
                                              RMS_EPS_128),
                                          waits=[ev[proj + "sq"]], inc="VE")
                            se_n = S.emit("scalar", lambda e, ss=ss, par=par:
                                          e.activation(ss[:, par, :], ss[:, par, :],
                                                       AF.Sqrt),
                                          waits=[("VE", ve_n)], inc="SE")
                            ve_n = S.emit("vector", lambda e, ss=ss, rs=rs, par=par,
                                          par8=par8:
                                          e.reciprocal(rs[:, par8, :], ss[:, par, :]),
                                          waits=[("SE", se_n)], inc="VE")
                            rr[proj] = ve_n
                            last_rr[(proj, par)] = ve_n

                        rope_defer.append(dict(blk=blk, par8=par8, qks=qks,
                                               qsb=qsb, ksb=ksb,
                                               evq=ev["q"], evk=ev["k"],
                                               rrq=rr["q"], rrk=rr["k"]))
                        if t + 1 < NT:
                            emit_ln_block(t + 1, b)
                            if b >= 1:
                                emit_tr_block(t + 1, b - 1)
                    if t + 1 < NT:
                        emit_tr_block(t + 1, 3)
                    xnT_pool.release(xnts, ("TE", S.cnt["TE"]))
                    if 1 <= t <= 6:
                        for j in range(6):
                            emit_conv_chunk()
                    if t == 1:
                        pos_ld_n[1] = dma("sync", pos_sb[:, :],
                                          pos_c[0:1, 2048:4096],
                                          waits=[("TE", rad_te[1])], inc="POS")
                    if t < 3:
                        emit_table_chunk(t + 1)

                emit_ropes()

                if "d_xn" in dbg_h:
                    dma("gpsimd", dbg_h["d_xn"], xn_d,
                        waits=[("XNS0", S.cnt.get("XNS0", 0)),
                               ("XNS1", S.cnt.get("XNS1", 0))], inc="DBG")


                if "d_cos" in dbg_h:
                    dma("sync", dbg_h["d_cos"], flat2(cos_sb[:, :, :]),
                        waits=[("SE", table_se[3])], inc="DBG")
                    dma("sync", dbg_h["d_sin"], flat2(sin_sb[:, :, :]),
                        waits=[("SE", table_se[3])], inc="DBG")
                if "d_wv" in dbg_h:
                    dma("sync", dbg_h["d_wv"], flat2(wv_sb[:, :, :]),
                        waits=[("WVL", wvld)], inc="DBG")

                S.flush()

            with ExitStack() as s2:
              if 2 in DEBUG_PHASES:
                  def A2(name, shape, dt):
                      return s2.enter_context(nc.sbuf_tensor(name, shape, dt))

                  kT_sb = A2("kT_sb", [128, H, T], BF16)
                  wo_sb = A2("wo_sb", [128, H, D], BF16)
                  qT_bufs = [A2(f"qT{i}", [128, H, 512], BF16) for i in range(2)]
                  pt_bufs = [A2(f"pt{i}", [128, 512], BF16) for i in range(4)]
                  otn_bufs = [A2(f"otn{i}", [128, H, 512], BF16) for i in range(2)]
                  x2_bufs = [A2(f"x2{i}", [128, D], F32) for i in range(3)]
                  o1_bufs = [A2(f"o1{i}", [128, D], F32) for i in range(2)]
                  mask_sb = A2("mask_sb", [128, 896], BF16)
                  lrb_sb = A2("lrb_sb", [8, 512], BF16)
                  lr8_sb = A2("lr8_sb", [8, 512], F32)
                  oneh_sb = A2("oneh_sb", [128, 136], BF16)
                  onehb_sb = A2("onehb_sb", [8, 1024], BF16)

                  pass

                  dma("sync", mask_sb[:, :], maskb_c, inc="CST")
                  dma("sync", oneh_sb[:, :], oneh_c, inc="CST")
                  cst2_n = dma("sync", onehb_sb[:, :], onehb_c, inc="CST")
                  wold = dma("gpsimd", wo_sb[:, :, :],
                             wo_h.rearrange("(h p) m -> p h m", p=128), inc="WOL")

                  qT_pool = Pool([0, 1])
                  pt_pool = Pool([0, 1, 2, 3])
                  otn_pool = Pool([0, 1])
                  x2_pool = Pool([0, 1, 2])
                  o1_pool = Pool([0, 1])
                  out1_cnt = [0, 0]
                  qt_ready = {}
                  kt_ready = {}
                  x2_ready = {}

                  def load_qT(qt):
                      slot, rel = qT_pool.acquire()
                      w = [("QRS", qr_mark[qt])]
                      n = None
                      for h in range(H):
                          n = dmat(qT_bufs[slot][:, h, :],
                                   qr_d[qt * 512:(qt + 1) * 512,
                                        h * 128:(h + 1) * 128],
                                   waits=(w + rel) if h == 0 else [],
                                   inc=f"QTL{slot}")
                      qt_ready[qt] = (slot, n)

                  def load_x2(blk):
                      slot, rel = x2_pool.acquire()
                      n = dma("sync", x2_bufs[slot][:, :],
                              x_h[blk * 128:(blk + 1) * 128, :],
                              waits=rel, inc=f"X2L{slot}")
                      x2_ready[blk] = (slot, n)

                  load_qT(0)
                  for half in range(2):
                      for h in range(H):
                          w = ([("KRS", kr_mark[3 if half == 0 else NT - 1])]
                               if h == 0 else [])
                          kt_ready[(h, half)] = dmat(
                              kT_sb[:, h, half * 2048:(half + 1) * 2048],
                              kr_d[half * 2048:(half + 1) * 2048,
                                   h * 128:(h + 1) * 128],
                              waits=w, inc=f"KTL{h}H{half}")
                  for blk in range(3):
                      load_x2(blk)

                  def emit_lo(st):
                      te_o = S.emit("tensor",
                                    lambda e, ob=st["ob"], sl=st["pt"], kc=st["kc"],
                                    h=st["h"], km=st["km"]: e.matmul(
                                        pb[ob][:, :],
                                        v_sb[:, kc, h * 128:(h + 1) * 128],
                                        pt_bufs[sl][:, :],
                                        start=(kc == 0), stop=(kc == km - 1)),
                                    waits=[("SE", st["se"]), ("CST", cst2_n)] +
                                          st["w0"],
                                    inc="TE" if st["kc"] == st["km"] - 1 else None)
                      te_l = S.emit("tensor",
                                    lambda e, lb=st["lb"], sl=st["pt"], kc=st["kc"],
                                    km=st["km"], h=st["h"]:
                                    e.matmul(pb[lb][0:128, :],
                                             oneh_sb[:, 7 - h:135 - h],
                                             pt_bufs[sl][:, :],
                                             start=(kc == 0 and h == 0),
                                             stop=(kc == km - 1 and h == H - 1)),
                                    inc="TE" if (st["kc"] == st["km"] - 1 and
                                                 st["h"] == H - 1) else None)
                      pt_pool.release(st["pt"], ("TE", S.cnt["TE"]))
                      return te_l, te_o

                  def emit_fin(fin, otn):
                      h, l_bank, o_bank, te_l, te_o = fin
                      se_o = S.emit("scalar", lambda e, ob=o_bank, h=h,
                                    otn=otn: e.copy(
                          otn[:, h, :], pb[ob][:, :]),
                          waits=[("TE", te_o if te_l is None
                                  else max(te_l, te_o))], inc="SE")
                      psum.release(o_bank, ("SE", se_o))
                      return se_o

                  def emit_norm(l_bank, otn, se_o_last):
                      se_n = S.emit("scalar", lambda e, lb=l_bank: e.activation(
                          lr8_sb[:, :], pb[lb][0:8, :], AF.Ln),
                          waits=[("SE", se_o_last)], inc="SE")
                      ve_n = S.emit("vector", lambda e: e.engine_nop(),
                                    waits=[("SE", se_n)], inc="VE")
                      psum.release(l_bank, ("VE", ve_n))
                      se_n = S.emit("scalar", lambda e: e.activation(
                          lrb_sb[:, :], lr8_sb[:, :], AF.Exp, scale=-1.0),
                          inc="SE")
                      otn_done = None
                      per_h = {}
                      for h in range(H):
                          b_bank, brel = psum.acquire()
                          te_b = S.emit("tensor", lambda e, bb=b_bank, h=h:
                                        e.matmul(pb[bb][:, :],
                                                 onehb_sb[:, h * 128:(h + 1) * 128],
                                                 lrb_sb[:, :],
                                                 start=True, stop=True),
                                        waits=[("SE", se_n)] + brel, inc="TE")
                          otn_done = S.emit("vector",
                                            lambda e, bb=b_bank, h=h, otn=otn:
                                            e.tensor_mul(otn[:, h, :],
                                                         otn[:, h, :],
                                                         pb[bb][:, :]),
                                            waits=[("TE", te_b)], inc="VE")
                          psum.release(b_bank, ("VE", otn_done))
                          per_h[h] = otn_done
                      return otn_done, per_h

                  for qt in range(NT):
                      if qt + 1 < NT:
                          load_qT(qt + 1)
                      qslot, qn = qt_ready[qt]
                      qT = qT_bufs[qslot]
                      oslot, orel = otn_pool.acquire()
                      otn = otn_bufs[oslot]
                      kc_max = 4 * qt + 4
                      otn_done = None
                      otn_ve = {}
                      pendq = deque()
                      pend_fin = None
                      l_bank, lrel = psum.acquire()
                      for h in range(H):
                          o_bank, orel2 = psum.acquire()
                          te_l = te_o = None
                          for kc in range(kc_max):
                              s_bank, srel = psum.acquire()
                              w = [(f"QTL{qslot}", qn),
                                   (f"KTL{h}H{kc // 16}",
                                    kt_ready[(h, kc // 16)])] + srel
                              if kc == 0 and h == 0:
                                  w += orel
                              te_s = S.emit("tensor",
                                            lambda e, sb=s_bank, h=h, kc=kc, qT=qT:
                                            e.matmul(
                                                pb[sb][:, :],
                                                kT_sb[:, h, kc * 128:(kc + 1) * 128],
                                                qT[:, h, :], start=True, stop=True),
                                            waits=w, inc="TE")
                              if len(pendq) >= 3:
                                  te_l, te_o = emit_lo(pendq.popleft())
                              if pend_fin is not None and (
                                      kc >= min(4, kc_max - 1)):
                                  emit_fin(pend_fin, otn)
                                  pend_fin = None
                              m = kc - 4 * qt
                              if m >= 0:
                                  ve_m = S.emit("vector", lambda e, sb=s_bank, m=m:
                                                e.tensor_add(
                                                    pb[sb][:, :], pb[sb][:, :],
                                                    mask_sb[:, 384 - 128 * m:
                                                            896 - 128 * m]),
                                                waits=[("TE", te_s),
                                                       ("CST", cst2_n)], inc="VE")
                                  expw = [("VE", ve_m)]
                              else:
                                  expw = [("TE", te_s)]
                              ptslot, ptrel = pt_pool.acquire()
                              se_e = S.emit("scalar", lambda e, sb=s_bank, sl=ptslot:
                                            e.activation(pt_bufs[sl][:, :],
                                                         pb[sb][:, :], AF.Exp),
                                            waits=expw + ptrel, inc="SE")
                              psum.release(s_bank, ("SE", se_e))
                              pendq.append({"lb": l_bank, "ob": o_bank,
                                            "pt": ptslot, "kc": kc, "km": kc_max,
                                            "h": h, "se": se_e,
                                            "w0": (lrel + orel2) if kc == 0
                                            else []})
                          while pendq:
                              te_l, te_o = emit_lo(pendq.popleft())
                          if pend_fin is not None:
                              emit_fin(pend_fin, otn)
                          pend_fin = (h, l_bank, o_bank, te_l, te_o)
                      se_o_last = emit_fin(pend_fin, otn)
                      pend_fin = None
                      otn_done, otn_ve = emit_norm(l_bank, otn, se_o_last)
                      qT_pool.release(qslot, ("TE", S.cnt["TE"]))

                      for tb in range(4):
                          blk = qt * 4 + tb
                          pair = []
                          relw = []
                          for half in range(2):
                              bank, rel = psum.acquire()
                              pair.append(bank)
                              relw += rel
                          te_p = None
                          for h in range(H):
                              for half in range(2):
                                  w = []
                                  if half == 0:
                                      w = [("VE", otn_ve[h])]
                                  if h == 0 and half == 0:
                                      w += [("WOL", wold)] + relw
                                  te_p = S.emit("tensor",
                                                lambda e, pk=pair[half], h=h,
                                                half=half, tb=tb, otn=otn: e.matmul(
                                                    pb[pk][:, :],
                                                    otn[:, h, tb * 128:(tb + 1) * 128],
                                                    wo_sb[:, h,
                                                          half * 512:(half + 1) * 512],
                                                    start=(h == 0), stop=(h == H - 1)),
                                                waits=w,
                                                inc="TE" if (h == H - 1 and half == 1)
                                                else None)
                          x2s, x2n = x2_ready[blk]
                          o1s, o1rel = o1_pool.acquire()
                          w = [("TE", te_p), (f"X2L{x2s}", x2n),
                               (f"OD1_{o1s}", out1_cnt[o1s])] + o1rel
                          S.emit("vector", lambda e, pk=pair[0], xs=x2s, os=o1s:
                                 e.tensor_add(o1_bufs[os][:, 0:512], pb[pk][:, :],
                                              x2_bufs[xs][:, 0:512]), waits=w)
                          ve_n = S.emit("vector",
                                        lambda e, pk=pair[1], xs=x2s, os=o1s:
                                        e.tensor_add(o1_bufs[os][:, 512:1024],
                                                     pb[pk][:, :],
                                                     x2_bufs[xs][:, 512:1024]),
                                        inc="VE")
                          x2_pool.release(x2s, ("VE", ve_n))
                          psum.release(pair[0], ("VE", ve_n))
                          psum.release(pair[1], ("VE", ve_n))
                          out1_cnt[o1s] = dma("gpsimd",
                                              out_h[blk * 128:(blk + 1) * 128, :],
                                              o1_bufs[o1s][:, :],
                                              waits=[("VE", ve_n)], inc=f"OD1_{o1s}")
                          if "d_o1" in dbg_h:
                              out1_cnt[o1s] = dma(
                                  "gpsimd", dbg_h["d_o1"][blk * 128:(blk + 1) * 128, :],
                                  o1_bufs[o1s][:, :], inc=f"OD1_{o1s}")
                          o1_pool.release(o1s, (f"OD1_{o1s}", out1_cnt[o1s]))
                          if blk + 3 < NB:
                              load_x2(blk + 3)
                      otn_pool.release(oslot, ("TE", S.cnt["TE"]))

                  if "d_v" in dbg_h:
                      for blk in range(NB):
                          dma("sync", dbg_h["d_v"][blk * 128:(blk + 1) * 128, :],
                              v_sb[:, blk, :], inc="DBG")

                  while conv_chunks:
                      emit_conv_chunk()

                  S.flush()

        with ExitStack() as s3:
          if 3 in DEBUG_PHASES:
              def A3(name, shape, dt):
                  return s3.enter_context(nc.sbuf_tensor(name, shape, dt))

              wgT_sb = A3("wgT_sb", [128, NDC, FF], BF16)
              wuT_sb = A3("wuT_sb", [128, NDC, FF], BF16)
              wdT_sb = A3("wdT_sb", [128, NHC, D], BF16)
              xnT3_bufs = [A3(f"xnT3_{i}", [128, NDC, 512], BF16) for i in range(2)]
              h_sb = A3("h_sb", [128, NHC, 512], BF16)
              g_bufs = [A3(f"g{i}", [128, 512], BF16) for i in range(2)]
              o13_bufs = [A3(f"o13_{i}", [128, D], F32) for i in range(3)]
              os_bufs = [A3(f"os{i}", [128, D], F32) for i in range(2)]

              conv_n = conv_total[0]
              xn_tot = [S.cnt.get("XNS0", 0), S.cnt.get("XNS1", 0)]
              od1_tot = [S.cnt.get("OD1_0", 0), S.cnt.get("OD1_1", 0)]

              xnT3_pool = Pool([0, 1])
              g_pool = Pool([0, 1])
              o13_pool = Pool([0, 1, 2])
              os_pool = Pool([0, 1])
              out_cnt = [0, 0]
              o13_ready = {}
              xnt3_ready = {}

              def load_xnT3(t):
                  slot, rel = xnT3_pool.acquire()
                  w = ([("XNS0", xn_tot[0]), ("XNS1", xn_tot[1])] if t == 0 else [])
                  n = None
                  for dc in range(NDC):
                      n = dmat(xnT3_bufs[slot][:, dc, :],
                               xn_d[t * 512:(t + 1) * 512, dc * 128:(dc + 1) * 128],
                               waits=(w + rel) if dc == 0 else [],
                               inc=f"XT3{slot}")
                  xnt3_ready[t] = (slot, n)

              def load_o13(blk):
                  slot, rel = o13_pool.acquire()
                  w = ([("OD1_0", od1_tot[0]), ("OD1_1", od1_tot[1])]
                       if blk == 0 else [])
                  n = dma("sync", o13_bufs[slot][:, :],
                          out_h[blk * 128:(blk + 1) * 128, :],
                          waits=w + rel, inc=f"O13{slot}")
                  o13_ready[blk] = (slot, n)

              wg_n = wu_n = wd_n = None
              for dc in range(NDC):
                  w = [("CONV", conv_n)] if dc == 0 else []
                  wg_n = dmat(wgT_sb[:, dc, :], wg_b[:, dc * 128:(dc + 1) * 128],
                              waits=w, inc="WGT")
              load_xnT3(0)
              for dc in range(NDC):
                  wu_n = dmat(wuT_sb[:, dc, :], wu_b[:, dc * 128:(dc + 1) * 128],
                              inc="WUT")
              for blk in range(3):
                  load_o13(blk)
              for hc in range(NHC):
                  wd_n = dmat(wdT_sb[:, hc, :], wd_b[:, hc * 128:(hc + 1) * 128],
                              inc="WDT")

              for t in range(NT):
                  slot, xnt_n = xnt3_ready[t]
                  xnT = xnT3_bufs[slot]

                  hc_done = None
                  for hc in range(NHC):
                      gb, grel = psum.acquire()
                      ub, urel = psum.acquire()
                      te_g = te_u = None
                      for dc in range(NDC):
                          w = []
                          if dc == 0:
                              w = [(f"XT3{slot}", xnt_n),
                                   ("WGT", 16)] + grel
                          if t == 0:
                              w += [("WGT", 16 * (dc + 1))]
                          te_g = S.emit("tensor",
                                        lambda e, gb=gb, dc=dc, hc=hc, xnT=xnT:
                                        e.matmul(
                                            pb[gb][:, :],
                                            wgT_sb[:, dc, hc * 128:(hc + 1) * 128],
                                            xnT[:, dc, :],
                                            start=(dc == 0), stop=(dc == NDC - 1)),
                                        waits=w, inc="TE" if dc == NDC - 1 else None)
                      for dc in range(NDC):
                          w = []
                          if dc == 0:
                              w = [("WUT", 16)] + urel
                          if t == 0:
                              w += [("WUT", 16 * (dc + 1))]
                          te_u = S.emit("tensor",
                                        lambda e, ub=ub, dc=dc, hc=hc, xnT=xnT:
                                        e.matmul(
                                            pb[ub][:, :],
                                            wuT_sb[:, dc, hc * 128:(hc + 1) * 128],
                                            xnT[:, dc, :],
                                            start=(dc == 0), stop=(dc == NDC - 1)),
                                        waits=w, inc="TE" if dc == NDC - 1 else None)
                      gs, gsrel = g_pool.acquire()
                      se_n = S.emit("scalar", lambda e, gb=gb, gs=gs: e.activation(
                          g_bufs[gs][:, :], pb[gb][:, :], AF.Silu, scale=INV127),
                          waits=[("TE", te_g)] + gsrel, inc="SE")
                      psum.release(gb, ("SE", se_n))
                      hc_done = S.emit("vector", lambda e, ub=ub, gs=gs, hc=hc:
                                       e.scalar_tensor_tensor(
                                           h_sb[:, hc, :], pb[ub][:, :], INV127,
                                           g_bufs[gs][:, :], ALU.mult, ALU.mult),
                                       waits=[("TE", te_u), ("SE", se_n)], inc="VE")
                      psum.release(ub, ("VE", hc_done))
                      g_pool.release(gs, ("VE", hc_done))

                  xnT3_pool.release(slot, ("TE", S.cnt["TE"]))
                  if t + 1 < NT:
                      load_xnT3(t + 1)

                  for tb in range(4):
                      blk = 4 * t + tb
                      pair = []
                      relw = []
                      for half in range(2):
                          bank, rel = psum.acquire()
                          pair.append(bank)
                          relw += rel
                      te_d = None
                      for hc in range(NHC):
                          for half in range(2):
                              w = []
                              if hc == 0 and half == 0:
                                  w = [("VE", hc_done), ("WDT", 16)] + relw
                              if t == 0 and half == 0:
                                  w += [("WDT", 16 * (hc + 1))]
                              te_d = S.emit("tensor",
                                            lambda e, pk=pair[half], hc=hc, tb=tb,
                                            half=half: e.matmul(
                                                pb[pk][:, :],
                                                h_sb[:, hc, tb * 128:(tb + 1) * 128],
                                                wdT_sb[:, hc,
                                                       half * 512:(half + 1) * 512],
                                                start=(hc == 0), stop=(hc == NHC - 1)),
                                            waits=w,
                                            inc="TE" if (hc == NHC - 1 and half == 1)
                                            else None)
                      o13s, o13n = o13_ready[blk]
                      oss, osrel = os_pool.acquire()
                      w = [("TE", te_d), (f"O13{o13s}", o13n),
                           (f"ODF{oss}", out_cnt[oss])] + osrel
                      S.emit("vector", lambda e, pk=pair[0], o13s=o13s, oss=oss:
                             e.scalar_tensor_tensor(
                                 os_bufs[oss][:, 0:512], pb[pk][:, :], INV127,
                                 o13_bufs[o13s][:, 0:512], ALU.mult, ALU.add),
                             waits=w)
                      ve_n = S.emit("vector",
                                    lambda e, pk=pair[1], o13s=o13s, oss=oss:
                                    e.scalar_tensor_tensor(
                                        os_bufs[oss][:, 512:1024], pb[pk][:, :],
                                        INV127, o13_bufs[o13s][:, 512:1024],
                                        ALU.mult, ALU.add),
                                    inc="VE")
                      o13_pool.release(o13s, ("VE", ve_n))
                      psum.release(pair[0], ("VE", ve_n))
                      psum.release(pair[1], ("VE", ve_n))
                      out_cnt[oss] = dma("gpsimd", out_h[blk * 128:(blk + 1) * 128, :],
                                         os_bufs[oss][:, :], waits=[("VE", ve_n)],
                                         inc=f"ODF{oss}")
                      os_pool.release(oss, (f"ODF{oss}", out_cnt[oss]))
                      if blk + 3 < NB:
                          load_o13(blk + 3)

              for nm, srcbuf in (("d_xn", xn_d), ("d_qr", qr_d), ("d_kr", kr_d)):
                  if nm in dbg_h:
                      dma("gpsimd", dbg_h[nm], srcbuf, inc="DBG")

              S.emit("gpsimd", lambda e: e.engine_nop(),
                     waits=[("ODF0", out_cnt[0]), ("ODF1", out_cnt[1])])

              S.flush()

    return nc


def kernel(L0_attn_x, L0_attn_wq, L0_attn_wk, L0_attn_wv, L0_attn_wo,
           L0_ffn_w_gate, L0_ffn_w_up, L0_ffn_w_down):
    nc = build_program()
    wq = np.ascontiguousarray(np.asarray(L0_attn_wq).reshape(D, D))
    wk = np.ascontiguousarray(np.asarray(L0_attn_wk).reshape(D, D))
    wv = np.ascontiguousarray(np.asarray(L0_attn_wv).reshape(D, D))
    wo = np.ascontiguousarray(np.asarray(L0_attn_wo).reshape(D, D))
    wg = np.ascontiguousarray(np.asarray(L0_ffn_w_gate))
    wu = np.ascontiguousarray(np.asarray(L0_ffn_w_up))
    wd = np.ascontiguousarray(np.asarray(L0_ffn_w_down))
    x = np.asarray(L0_attn_x)
    in_maps = [{"x": np.ascontiguousarray(x[b]), "wq": wq, "wk": wk, "wv": wv,
                "wo": wo, "wg": wg, "wu": wu, "wd": wd} for b in range(8)]
    res = run_bass_kernel_spmd(nc, in_maps, core_ids=list(range(8)))
    return np.stack([res.results[b]["out"] for b in range(8)], axis=0)


# version 37
# speedup vs baseline: 1.1501x; 1.1501x over previous
import math
from collections import deque
from contextlib import ExitStack

import numpy as np
import concourse.bass as bass
import concourse.mybir as mybir
from concourse.bass_utils import run_bass_kernel_spmd

F32 = mybir.dt.float32
BF16 = mybir.dt.bfloat16
I32 = mybir.dt.int32
AF = mybir.ActivationFunctionType
ALU = mybir.AluOpType

T, D, H, DH, HLF, FF = 4096, 1024, 8, 128, 64, 3072
NB = T // 128
NT = T // 512
NHC = FF // 128
NDC = D // 128

LN_EPS = 1e-5
RMS_EPS_128 = 128.0 * 1e-6
INV127 = 1.0 / 127.0
NEG_BIG = -1.0e30

TWO_PI = 2.0 * math.pi
TPI_HI = float(np.float32(TWO_PI))
TPI_LO = float(np.float64(TWO_PI) - np.float64(np.float32(TWO_PI)))
PI_2 = math.pi / 2.0
P2_HI = float(np.float32(PI_2))
P2_LO = float(np.float64(PI_2) - np.float64(np.float32(PI_2)))
INV2PI = 1.0 / TWO_PI
PI_CLAMP = float(np.nextafter(np.float32(math.pi), np.float32(0.0)))

ENGINES = ("tensor", "vector", "scalar", "sync", "gpsimd")


class Sched:

    def __init__(self, nc):
        self.nc = nc
        self.items = {e: [] for e in ENGINES}
        self._sems = {}
        self.cnt = {}

    def sem(self, name):
        if name not in self._sems:
            self._sems[name] = self.nc.alloc_semaphore(name=name)
            self.cnt[name] = 0
        return self._sems[name]

    def emit(self, eng, fn, waits=(), inc=None, inc_by=1):
        if SERIAL:
            waits = list(waits) + [(s, v) for s, v in self.cnt.items()]
        waits = [(s, v) for (s, v) in waits if v > 0]
        for s, v in waits:
            assert s in self.cnt and v <= self.cnt[s], (
                f"wait on future value: {s} {v} > {self.cnt.get(s)}")
        if inc is not None:
            self.sem(inc)
            self.cnt[inc] += inc_by
        self.items[eng].append((waits, fn, inc, inc_by))
        return self.cnt[inc] if inc is not None else None

    def flush(self):
        nc = self.nc
        active = [e for e in ENGINES if self.items[e]]
        if not active:
            return
        own = {"tensor": "TE", "vector": "VE", "scalar": "SE", "gpsimd": "GE"}
        with nc.Block() as block:
            for e in active:
                items = self.items[e]

                def body(eng, items=items, e=e):
                    waited = {}
                    own_cnt = 0
                    for waits, fn, inc, inc_by in items:
                        for s, v in waits:
                            if s == own.get(e) and v <= own_cnt:
                                continue
                            if waited.get(s, -1) >= v:
                                continue
                            waited[s] = v
                            eng.wait_ge(self._sems[s], v)
                        inst = fn(eng)
                        if inc is not None:
                            assert inst is not None
                            inst.then_inc(self._sems[inc], inc_by)
                            if inc == own.get(e):
                                own_cnt += inc_by

                getattr(block, e)(body)
        for e in ENGINES:
            self.items[e] = []


class Pool:

    def __init__(self, resources):
        self.q = deque((r, []) for r in resources)

    def acquire(self):
        r, rel = self.q.popleft()
        return r, list(rel)

    def release(self, r, rel):
        if isinstance(rel, tuple):
            rel = [rel]
        self.q.append((r, rel or []))


def bcast_free(ap, reps, where=1):
    dims = [list(d) for d in ap.ap]
    dims.insert(where, [0, reps])
    return bass.AP(ap.tensor, ap.offset, dims)


def flat2(ap3):
    return ap3.rearrange("p a b -> p (a b)")


DEBUG_PHASES = (1, 2, 3)
SERIAL = False


def build_program(dbg=None):
    nc = bass.Bass("TRN2", target_bir_lowering=False,
                   detect_race_conditions=False)
    dbg = dbg or []

    x_h = nc.dram_tensor("x", [T, D], F32, kind="ExternalInput").ap()
    wq_h = nc.dram_tensor("wq", [D, D], F32, kind="ExternalInput").ap()
    wk_h = nc.dram_tensor("wk", [D, D], F32, kind="ExternalInput").ap()
    wv_h = nc.dram_tensor("wv", [D, D], F32, kind="ExternalInput").ap()
    wo_h = nc.dram_tensor("wo", [D, D], F32, kind="ExternalInput").ap()
    wg_h = nc.dram_tensor("wg", [FF, D], I32, kind="ExternalInput").ap()
    wu_h = nc.dram_tensor("wu", [FF, D], I32, kind="ExternalInput").ap()
    wd_h = nc.dram_tensor("wd", [D, FF], I32, kind="ExternalInput").ap()
    out_h = nc.dram_tensor("out", [T, D], F32, kind="ExternalOutput").ap()

    xn_d = nc.dram_tensor("xn_d", [T, D], BF16, kind="Internal").ap()
    qr_d = nc.dram_tensor("qr_d", [T, D], BF16, kind="Internal").ap()
    kr_d = nc.dram_tensor("kr_d", [T, D], BF16, kind="Internal").ap()
    wg_b = nc.dram_tensor("wg_b", [FF, D], BF16, kind="Internal").ap()
    wu_b = nc.dram_tensor("wu_b", [FF, D], BF16, kind="Internal").ap()
    wd_b = nc.dram_tensor("wd_b", [D, FF], BF16, kind="Internal").ap()

    dbg_h = {}
    for name, shape, dt in dbg:
        dbg_h[name] = nc.dram_tensor(name, shape, dt, kind="ExternalOutput").ap()

    import ml_dtypes
    pos_np = np.arange(T, dtype=np.float32).reshape(1, T)
    dfrac_np = (-(np.arange(HLF, dtype=np.float64)) / HLF).astype(np.float32).reshape(1, HLF)
    ten4_np = np.full((1, 1), 10000.0, dtype=np.float32)
    jj = np.arange(128).reshape(128, 1)
    xx = np.arange(896).reshape(1, 896)
    maskb_np = np.where(jj > xx - 384, NEG_BIG, 0.0).astype(ml_dtypes.bfloat16)
    onesc_np = np.ones((128, 1), dtype=ml_dtypes.bfloat16)
    onesr_np = np.ones((1, 128), dtype=np.float32)
    onesrb_np = np.ones((1, 128), dtype=ml_dtypes.bfloat16)
    oneh_np = np.zeros((128, 136), dtype=ml_dtypes.bfloat16)
    oneh_np[:, 7] = 1
    onehb_np = np.zeros((8, 1024), dtype=ml_dtypes.bfloat16)
    for _h in range(8):
        onehb_np[_h, _h * 128:(_h + 1) * 128] = 1
    id_np = np.eye(128, dtype=ml_dtypes.bfloat16)

    pos_c = nc.inline_tensor(pos_np, name="pos_c").ap()
    dfrac_c = nc.inline_tensor(dfrac_np, name="dfrac_c").ap()
    ten4_c = nc.inline_tensor(ten4_np, name="ten4_c").ap()
    maskb_c = nc.inline_tensor(maskb_np, name="maskb_c").ap()
    onesc_c = nc.inline_tensor(onesc_np, name="onesc_c").ap()
    onesr_c = nc.inline_tensor(onesr_np, name="onesr_c").ap()
    onesrb_c = nc.inline_tensor(onesrb_np, name="onesrb_c").ap()
    oneh_c = nc.inline_tensor(oneh_np, name="oneh_c").ap()
    onehb_c = nc.inline_tensor(onehb_np, name="onehb_c").ap()
    id_c = nc.inline_tensor(id_np, name="id_c").ap()

    S = Sched(nc)

    def dma(queue, dst, src, waits=(), inc=None):
        return S.emit(queue, lambda e, d=dst, s=src: e.dma_start(d, s),
                      waits=waits, inc=inc, inc_by=16)

    def dmat(dst, src, waits=(), inc=None):
        return S.emit("sync", lambda e, d=dst, s=src: e.dma_start_transpose(d, s),
                      waits=waits, inc=inc, inc_by=16)

    with ExitStack() as top:
        pb = [top.enter_context(nc.psum_tensor(f"pb{i}", [128, 512], F32))
              for i in range(8)]
        psum = Pool(list(range(8)))

        with ExitStack() as s12:
            v_sb = s12.enter_context(nc.sbuf_tensor("v_sb", [128, NB, D], BF16))

            with ExitStack() as s1:
                def A(name, shape, dt):
                    return s1.enter_context(nc.sbuf_tensor(name, shape, dt))

                wq_sb = A("wq_sb", [128, NDC, D], BF16)
                wk_sb = A("wk_sb", [128, NDC, D], BF16)
                wv_sb = A("wv_sb", [128, NDC, D], BF16)
                x_bufs = [A(f"x{i}", [128, D], F32) for i in range(2)]
                xn_bufs = [A(f"xn{i}", [128, D], BF16) for i in range(2)]
                xnT_bufs = [A(f"xnT{i}", [128, NDC, 512], BF16) for i in range(2)]
                q_sb4 = A("q_sb4", [128, 4, D], BF16)
                k_sb4 = A("k_sb4", [128, 4, D], BF16)
                qr_pair = A("qr_pair", [128, 2, D], BF16)
                kr_pair = A("kr_pair", [128, 2, D], BF16)
                cos_sb = A("cos_sb", [128, NB, HLF], BF16)
                sin_sb = A("sin_sb", [128, NB, HLF], BF16)
                sqd = A("sqd", [128, D], F32)
                sqv = A("sqv", [128, D], BF16)
                ssx = A("ssx", [128, 4], F32)
                sux = A("sux", [128, 4], F32)
                mean_t = A("mean_t", [128, 4], F32)
                var_t = A("var_t", [128, 4], F32)
                tmp_t = A("tmp_t", [128, 4], F32)
                std_t = A("std_t", [128, 4], F32)
                rstd_t = A("rstd_t", [128, 4], F32)
                nmr_t = A("nmr_t", [128, 4], F32)
                ssq_t = A("ssq_t", [128, 4, H], F32)
                ssk_t = A("ssk_t", [128, 4, H], F32)
                rsq_t = A("rsq_t", [128, 8, H], F32)
                rsk_t = A("rsk_t", [128, 8, H], F32)
                rc_t = A("rc_t", [128, 2, H, HLF], BF16)
                rs_t = A("rs_t", [128, 2, H, HLF], BF16)
                rope_u = A("rope_u", [128, 2, H, HLF], BF16)
                rope_w = A("rope_w", [128, 2, H, HLF], BF16)
                pos_sb = A("pos_sb", [1, 2048], F32)
                dfrac_sb = A("dfrac_sb", [1, HLF], F32)
                ten4_sb = A("ten4_sb", [1, 1], F32)
                ln4_sb = A("ln4_sb", [1, 1], F32)
                efrq_sb = A("efrq_sb", [1, HLF], F32)
                frq_sb = A("frq_sb", [1, HLF], F32)
                tg_a = A("tg_a", [128, 512], F32)
                tg_r = A("tg_r", [128, 512], F32)
                tg_ki = A("tg_ki", [128, 512], I32)
                id_sb = A("id_sb", [128, 128], BF16)

                conv_chunks = []
                for (src, dst, rows) in ((wg_h, wg_b, FF), (wu_h, wu_b, FF),
                                         (wd_h, wd_b, D)):
                    step = 256 if rows == FF else 128
                    for r0 in range(0, rows, step):
                        conv_chunks.append((src, dst, r0, step))
                conv_total = [0]

                def emit_conv_chunk(queue="gpsimd", waits=None):
                    if conv_chunks:
                        src, dst, r0, step = conv_chunks.pop(0)
                        conv_total[0] = dma(queue, dst[r0:r0 + step, :],
                                            src[r0:r0 + step, :],
                                            waits=waits or [], inc="CONV")

                dma("sync", dfrac_sb[:, :], dfrac_c, inc="CST")
                pos_ld_n = {0: dma("sync", pos_sb[:, :],
                                    pos_c[0:1, 0:2048], inc="POS")}
                idl_n = dma("sync", id_sb[:, :], id_c, inc="CST")
                cst_n = dma("sync", ten4_sb[:, :], ten4_c, inc="CST")
                wqld = dma("gpsimd", wq_sb[:, :, :],
                           wq_h.rearrange("(c p) m -> p c m", p=128), inc="WQL")
                wkld = dma("gpsimd", wk_sb[:, :, :],
                           wk_h.rearrange("(c p) m -> p c m", p=128), inc="WKL")
                wvld = dma("gpsimd", wv_sb[:, :, :],
                           wv_h.rearrange("(c p) m -> p c m", p=128), inc="WVL")

                se_n = S.emit("scalar", lambda e: e.activation(
                    ln4_sb[:, :], ten4_sb[:, :], AF.Ln), waits=[("CST", cst_n)],
                    inc="SE")
                ve_n = S.emit("vector", lambda e: e.tensor_scalar_mul(
                    efrq_sb[:, :], dfrac_sb[:, :], ln4_sb[0:1, 0:1]),
                    waits=[("SE", se_n)], inc="VE")
                frq_ready = S.emit("scalar", lambda e: e.activation(
                    frq_sb[:, :], efrq_sb[:, :], AF.Exp), waits=[("VE", ve_n)],
                    inc="SE")

                last_tab_se = [0]

                def range_reduce(dst2d, rad_ap, extra_pi2, first_waits):
                    fw = list(first_waits) + [("SE", last_tab_se[0])]
                    if extra_pi2:
                        S.emit("vector", lambda e: e.tensor_scalar(
                            tg_r[:, :], rad_ap, P2_HI, P2_LO, ALU.add, ALU.add),
                            waits=fw)
                    else:
                        S.emit("vector", lambda e: e.tensor_copy(
                            tg_r[:, :], rad_ap), waits=fw)
                    rad_freed = S.emit("vector", lambda e: e.engine_nop(), inc="VE")
                    for _ in range(2):
                        S.emit("vector", lambda e: e.tensor_scalar(
                            tg_a[:, :], tg_r[:, :], INV2PI, 0.5,
                            ALU.mult, ALU.add))
                        S.emit("vector", lambda e: e.tensor_copy(
                            tg_ki[:, :], tg_a[:, :]))
                        S.emit("vector", lambda e: e.tensor_copy(
                            tg_a[:, :], tg_ki[:, :]))
                        S.emit("vector", lambda e: e.scalar_tensor_tensor(
                            tg_r[:, :], tg_a[:, :], -TPI_HI, tg_r[:, :],
                            ALU.mult, ALU.add))
                        S.emit("vector", lambda e: e.scalar_tensor_tensor(
                            tg_r[:, :], tg_a[:, :], -TPI_LO, tg_r[:, :],
                            ALU.mult, ALU.add))
                    S.emit("vector", lambda e: e.tensor_scalar_min(
                        tg_r[:, :], tg_r[:, :], PI_CLAMP))
                    v_n = S.emit("vector", lambda e: e.tensor_scalar_max(
                        tg_r[:, :], tg_r[:, :], -PI_CLAMP), inc="VE")
                    s_n = S.emit("scalar", lambda e, d=dst2d: e.activation(
                        d, tg_r[:, :], AF.Sin), waits=[("VE", v_n)], inc="SE")
                    last_tab_se[0] = s_n
                    return s_n, rad_freed

                table_se = [None] * 4
                rad_te = [0] * 4

                def emit_table_chunk(c):
                    bank, rel = psum.acquire()
                    te_n = None
                    for b in range(8):
                        w = ([("POS", pos_ld_n[c // 2]), ("SE", frq_ready)] +
                             (rel if b == 0 else []))
                        te_n = S.emit("tensor", lambda e, bk=bank, c=c, b=b:
                                      e.matmul(pb[bk][:, b * 64:(b + 1) * 64],
                                               pos_sb[0:1, (c % 2) * 1024 + b * 128:
                                                      (c % 2) * 1024 + (b + 1) * 128],
                                               frq_sb[0:1, :], start=True, stop=True),
                                      waits=w, inc="TE")
                    rad_te[c] = te_n
                    range_reduce(flat2(sin_sb[:, 8 * c:8 * c + 8, :]),
                                 pb[bank][:, :], False, [("TE", te_n)])
                    table_se[c], rad_freed = range_reduce(
                        flat2(cos_sb[:, 8 * c:8 * c + 8, :]), pb[bank][:, :],
                        True, [])
                    psum.release(bank, ("VE", rad_freed))

                x_pool = Pool([0, 1])
                xn_pool = Pool([0, 1])
                xnT_pool = Pool([0, 1])
                qk_pool = Pool([0, 1, 2, 3])
                qr_pool = Pool([0, 1])
                xnd_cnt = [0, 0]
                qrd_cnt = [0, 0]
                krd_cnt = [0, 0]
                last_rr = {}
                xnt_ready = {}

                emit_table_chunk(0)

                rope_defer = []
                qr_mark = {}
                kr_mark = {}
                xnt_first_rel = {}
                last_apply = {}
                qr_store = [0]
                kr_store = [0]

                def emit_ropes():
                    work, rope_defer[:] = list(rope_defer), []
                    if not work:
                        return
                    for a0, a1 in zip(work[0::2], work[1::2]):
                        blk = a0["blk"]
                        par8 = a0["par8"]
                        tchunk = blk // 8
                        for proj, sb4, rsb, dstp, dsem, dcnt, dd, evp, rrp in (
                                ("q", q_sb4, rsq_t, qr_pair, "QRS", qr_store,
                                 qr_d, max(a0["evq"], a1["evq"]),
                                 max(a0["rrq"], a1["rrq"])),
                                ("k", k_sb4, rsk_t, kr_pair, "KRS", kr_store,
                                 kr_d, max(a0["evk"], a1["evk"]),
                                 max(a0["rrk"], a1["rrk"]))):
                            b0 = blk % 4
                            s4 = sb4[:, b0:b0 + 2, :].rearrange(
                                "p b (h d) -> p b h d", d=DH)
                            d4 = dstp[:, :, :].rearrange(
                                "p b (h d) -> p b h d", d=DH)
                            rstd_b = bcast_free(rsb[:, par8:par8 + 2, :], HLF,
                                                where=3)
                            cosb = bcast_free(cos_sb[:, blk:blk + 2, :], H, where=2)
                            sinb = bcast_free(sin_sb[:, blk:blk + 2, :], H, where=2)
                            w0 = [("SE", evp), ("SE", table_se[tchunk]),
                                  ("VE", rrp), (dsem, dcnt[0])]
                            S.emit("vector", lambda e, rb=rstd_b, cb=cosb:
                                   e.tensor_mul(rc_t[:, :, :, :], rb, cb), waits=w0)
                            S.emit("vector", lambda e, rb=rstd_b, sb2=sinb:
                                   e.tensor_mul(rs_t[:, :, :, :], rb, sb2))
                            S.emit("vector", lambda e, s4=s4: e.tensor_mul(
                                rope_u[:, :, :, :], s4[:, :, :, 0:HLF],
                                rc_t[:, :, :, :]))
                            S.emit("vector", lambda e, s4=s4: e.tensor_mul(
                                rope_w[:, :, :, :], s4[:, :, :, HLF:DH],
                                rs_t[:, :, :, :]))
                            S.emit("vector", lambda e, d4=d4: e.tensor_sub(
                                d4[:, :, :, 0:HLF], rope_u[:, :, :, :],
                                rope_w[:, :, :, :]))
                            S.emit("vector", lambda e, s4=s4: e.tensor_mul(
                                rope_u[:, :, :, :], s4[:, :, :, 0:HLF],
                                rs_t[:, :, :, :]))
                            S.emit("vector", lambda e, s4=s4: e.tensor_mul(
                                rope_w[:, :, :, :], s4[:, :, :, HLF:DH],
                                rc_t[:, :, :, :]))
                            ve_n = S.emit("vector", lambda e, d4=d4: e.tensor_add(
                                d4[:, :, :, HLF:DH], rope_u[:, :, :, :],
                                rope_w[:, :, :, :]), inc="VE")
                            dst_h = dd[blk * 128:(blk + 2) * 128, :].rearrange(
                                "(b p) m -> p b m", p=128)
                            dcnt[0] = dma("gpsimd", dst_h, dstp[:, :, :],
                                          waits=[("VE", ve_n)], inc=dsem)
                            if proj == "k":
                                rel = ("VE", ve_n)
                                qk_pool.release(a0["qks"], rel)
                                qk_pool.release(a1["qks"], rel)
                    t_done = work[-1]["blk"] // 4
                    qr_mark[t_done] = qr_store[0]
                    kr_mark[t_done] = kr_store[0]

                ln_store_mark = {}
                ln_apply_mark = {}
                tr_mark = {}
                xnt_slot_of = {}
                xnt_blk_n = {}

                def emit_ln_block(t, b):
                        if b == 0:
                            xnts0, _xr = xnT_pool.acquire()
                            xnt_slot_of[t] = xnts0
                            if _xr:
                                xnt_first_rel[t] = _xr
                        if True:
                            blk = 4 * t + b
                            par = blk % 4
                            xslot, xrel = x_pool.acquire()
                            xn_ld = dma("sync", x_bufs[xslot][:, :],
                                        x_h[blk * 128:(blk + 1) * 128, :],
                                        waits=xrel, inc=f"XL{xslot}")
                            xb = x_bufs[xslot]
                            S.emit("scalar", lambda e, xb=xb, par=par: e.activation(
                                sqd[:, :], xb[:, :], AF.Square,
                                accum_out=ssx[:, par:par + 1]),
                                waits=[(f"XL{xslot}", xn_ld)])
                            se_n = S.emit("scalar", lambda e, xb=xb, par=par: e.activation(
                                sqd[:, :], xb[:, :], AF.Identity,
                                accum_out=sux[:, par:par + 1]), inc="SE")
                            S.emit("vector", lambda e, par=par: e.tensor_scalar_mul(
                                mean_t[:, par:par + 1], sux[:, par:par + 1], 1.0 / D),
                                waits=[("SE", se_n)])
                            S.emit("vector", lambda e, par=par: e.tensor_scalar_mul(
                                var_t[:, par:par + 1], ssx[:, par:par + 1], 1.0 / D))
                            S.emit("vector", lambda e, par=par: e.scalar_tensor_tensor(
                                tmp_t[:, par:par + 1], mean_t[:, par:par + 1], -1.0,
                                mean_t[:, par:par + 1], ALU.mult, ALU.mult))
                            S.emit("vector", lambda e, par=par: e.tensor_add(
                                var_t[:, par:par + 1], var_t[:, par:par + 1],
                                tmp_t[:, par:par + 1]))
                            ve_n = S.emit("vector", lambda e, par=par: e.tensor_scalar_add(
                                var_t[:, par:par + 1], var_t[:, par:par + 1], LN_EPS),
                                inc="VE")
                            se_n = S.emit("scalar", lambda e, par=par: e.activation(
                                std_t[:, par:par + 1], var_t[:, par:par + 1], AF.Sqrt),
                                waits=[("VE", ve_n)], inc="SE")
                            S.emit("vector", lambda e, par=par: e.reciprocal(
                                rstd_t[:, par:par + 1], std_t[:, par:par + 1]),
                                waits=[("SE", se_n)])
                            xnslot, xnrel = xn_pool.acquire()
                            w = [(f"XNS{xnslot}", xnd_cnt[xnslot])] + xnrel
                            ap_e, ap_n0 = "VE", S.emit(
                                "vector", lambda e, xb=xb, par=par, sl=xnslot:
                                e.tensor_scalar(
                                    xn_bufs[sl][:, :], xb[:, :],
                                    mean_t[:, par:par + 1],
                                    rstd_t[:, par:par + 1],
                                    ALU.subtract, ALU.mult),
                                waits=w, inc="VE")
                            ap_n = ap_n0
                            x_pool.release(xslot, ("VE", ap_n))
                            last_apply[par] = 0
                            xnd_cnt[xnslot] = dma(
                                "gpsimd", xn_d[blk * 128:(blk + 1) * 128, :],
                                xn_bufs[xnslot][:, :],
                                waits=[("VE", ap_n)], inc=f"XNS{xnslot}")
                            ln_apply_mark[(t, b)] = (xnslot, ap_n,
                                                     xnd_cnt[xnslot])

                        ln_store_mark[t] = (xnd_cnt[0], xnd_cnt[1])

                def emit_tr_block(t, b):
                    xnts = xnt_slot_of[t]
                    xnslot, ap_n, xnd_n = ln_apply_mark.pop((t, b))
                    se_last = None
                    for g in range(2):
                        bank, rel = psum.acquire()
                        te_g = None
                        for j in range(4):
                            dc = 4 * g + j
                            w = []
                            if j == 0:
                                w = [("VE", ap_n), ("CST", idl_n)] + rel
                            te_g = S.emit(
                                "tensor",
                                lambda e, bank=bank, j=j, dc=dc, sl=xnslot:
                                e.matmul(pb[bank][:, j * 128:(j + 1) * 128],
                                         xn_bufs[sl][:, dc * 128:(dc + 1) * 128],
                                         id_sb[:, :], start=True, stop=True),
                                waits=w, inc="TE" if j == 3 else None)
                        w2 = [("TE", te_g)]
                        if g == 0:
                            w2 += xnt_first_rel.pop(t, [])
                        se_last = S.emit(
                            "scalar",
                            lambda e, bank=bank, g=g, b=b, xnts=xnts:
                            e.copy(xnT_bufs[xnts][:, 4 * g:4 * g + 4,
                                                  b * 128:(b + 1) * 128],
                                   pb[bank][:, :].rearrange(
                                       "p (c k) -> p c k", k=128)),
                            waits=w2, inc="SE")
                        psum.release(bank, ("SE", se_last))
                    xn_pool.release(xnslot, [(f"XNS{xnslot}", xnd_n),
                                             ("TE", S.cnt["TE"])])
                    tr_mark[(t, b)] = se_last

                
                for b in range(4):
                    emit_ln_block(0, b)
                    if b >= 1:
                        emit_tr_block(0, b - 1)
                emit_tr_block(0, 3)

                for t in range(NT):
                    emit_ropes()

                    xnts = xnt_slot_of[t]
                    xnT = xnT_bufs[xnts]

                    for b in range(4):
                        blk = 4 * t + b
                        par = blk % 4
                        stops = {}
                        banks = {}
                        for proj, wsb, wldnm, wldn in (
                                ("q", wq_sb, "WQL", wqld),
                                ("k", wk_sb, "WKL", wkld),
                                ("v", wv_sb, "WVL", wvld)):
                            pair = []
                            relw = []
                            for half in range(2):
                                bank, rel = psum.acquire()
                                pair.append(bank)
                                relw += rel
                            te_n = None
                            for dc in range(NDC):
                                for half in range(2):
                                    w = []
                                    if dc == 0 and half == 0:
                                        w = [("SE", tr_mark[(t, b)]),
                                             (wldnm, wldn)] + relw
                                    te_n = S.emit(
                                        "tensor",
                                        lambda e, pk=pair[half], dc=dc, b=b,
                                        wsb=wsb, half=half, xnT=xnT: e.matmul(
                                            pb[pk][:, :],
                                            xnT[:, dc, b * 128:(b + 1) * 128],
                                            wsb[:, dc, half * 512:(half + 1) * 512],
                                            start=(dc == 0), stop=(dc == NDC - 1)),
                                        waits=w,
                                        inc="TE" if (dc == NDC - 1 and half == 1)
                                        else None)
                            stops[proj] = te_n
                            banks[proj] = pair

                        qks, qkrel = qk_pool.acquire()
                        qsb, ksb = q_sb4[:, qks, :], k_sb4[:, qks, :]
                        ev = {}
                        for proj, dst in (("q", qsb), ("k", ksb), ("v", None)):
                            pair = banks[proj]
                            se_n = None
                            for half in range(2):
                                if proj == "v":
                                    d = v_sb[:, blk, half * 512:(half + 1) * 512]
                                else:
                                    d = dst[:, half * 512:(half + 1) * 512]
                                w = [("TE", stops[proj])]
                                if proj == "q" and half == 0:
                                    w += qkrel
                                se_n = S.emit("scalar",
                                              lambda e, d=d, pk=pair[half]:
                                              e.copy(d, pb[pk][:, :]),
                                              waits=w, inc="SE")
                            ev[proj] = se_n

                        if "d_q" in dbg_h:
                            dma("gpsimd", dbg_h["d_q"][blk * 128:(blk + 1) * 128, :],
                                qsb[:, :], waits=[("SE", ev["q"])], inc="DBG")
                            dma("gpsimd", dbg_h["d_k"][blk * 128:(blk + 1) * 128, :],
                                ksb[:, :], waits=[("SE", ev["k"])], inc="DBG")
                        for proj in ("q", "k", "v"):
                            pair = banks[proj]
                            relsem = ("SE", ev[proj])
                            psum.release(pair[0], relsem)
                            psum.release(pair[1], relsem)
                        for proj, dst, sb0 in (("q", ssq_t, qsb), ("k", ssk_t, ksb)):
                            w = [("VE", last_rr.get((proj, par), 0)),
                                 ("SE", ev[proj])]
                            S.emit("vector", lambda e, sb0=sb0:
                                   e.tensor_mul(sqv[:, :], sb0, sb0), waits=w)
                            ve_n = S.emit("vector", lambda e, dst=dst, par=par:
                                          e.reduce_sum(
                                              dst[:, par, :],
                                              sqv[:, :].rearrange(
                                                  "p (h d) -> p h d", d=DH),
                                              axis=mybir.AxisListType.X),
                                          inc="VE")
                            ev[proj + "sq"] = ("VE", ve_n)

                        rr = {}
                        par8 = blk % 8
                        for proj, ss, rs in (("q", ssq_t, rsq_t), ("k", ssk_t, rsk_t)):
                            ve_n = S.emit("vector", lambda e, ss=ss, par=par:
                                          e.tensor_scalar_add(
                                              ss[:, par, :], ss[:, par, :],
                                              RMS_EPS_128),
                                          waits=[ev[proj + "sq"]], inc="VE")
                            se_n = S.emit("scalar", lambda e, ss=ss, par=par:
                                          e.activation(ss[:, par, :], ss[:, par, :],
                                                       AF.Sqrt),
                                          waits=[("VE", ve_n)], inc="SE")
                            ve_n = S.emit("vector", lambda e, ss=ss, rs=rs, par=par,
                                          par8=par8:
                                          e.reciprocal(rs[:, par8, :], ss[:, par, :]),
                                          waits=[("SE", se_n)], inc="VE")
                            rr[proj] = ve_n
                            last_rr[(proj, par)] = ve_n

                        rope_defer.append(dict(blk=blk, par8=par8, qks=qks,
                                               qsb=qsb, ksb=ksb,
                                               evq=ev["q"], evk=ev["k"],
                                               rrq=rr["q"], rrk=rr["k"]))
                        if t + 1 < NT:
                            emit_ln_block(t + 1, b)
                            if b >= 1:
                                emit_tr_block(t + 1, b - 1)
                    if t + 1 < NT:
                        emit_tr_block(t + 1, 3)
                    xnT_pool.release(xnts, ("TE", S.cnt["TE"]))
                    if 1 <= t <= 6:
                        for j in range(6):
                            emit_conv_chunk()
                    if t == 1:
                        pos_ld_n[1] = dma("sync", pos_sb[:, :],
                                          pos_c[0:1, 2048:4096],
                                          waits=[("TE", rad_te[1])], inc="POS")
                    if t < 3:
                        emit_table_chunk(t + 1)

                emit_ropes()

                if "d_xn" in dbg_h:
                    dma("gpsimd", dbg_h["d_xn"], xn_d,
                        waits=[("XNS0", S.cnt.get("XNS0", 0)),
                               ("XNS1", S.cnt.get("XNS1", 0))], inc="DBG")


                if "d_cos" in dbg_h:
                    dma("sync", dbg_h["d_cos"], flat2(cos_sb[:, :, :]),
                        waits=[("SE", table_se[3])], inc="DBG")
                    dma("sync", dbg_h["d_sin"], flat2(sin_sb[:, :, :]),
                        waits=[("SE", table_se[3])], inc="DBG")
                if "d_wv" in dbg_h:
                    dma("sync", dbg_h["d_wv"], flat2(wv_sb[:, :, :]),
                        waits=[("WVL", wvld)], inc="DBG")

                S.flush()

            with ExitStack() as s2:
              if 2 in DEBUG_PHASES:
                  def A2(name, shape, dt):
                      return s2.enter_context(nc.sbuf_tensor(name, shape, dt))

                  kT_sb = A2("kT_sb", [128, H, T], BF16)
                  wo_sb = A2("wo_sb", [128, H, D], BF16)
                  qT_bufs = [A2(f"qT{i}", [128, H, 512], BF16) for i in range(2)]
                  pt_bufs = [A2(f"pt{i}", [128, 512], BF16) for i in range(4)]
                  otn_bufs = [A2(f"otn{i}", [128, H, 512], BF16) for i in range(2)]
                  x2_bufs = [A2(f"x2{i}", [128, D], F32) for i in range(3)]
                  o1_bufs = [A2(f"o1{i}", [128, D], F32) for i in range(2)]
                  mask_sb = A2("mask_sb", [128, 896], BF16)
                  lrb_sb = A2("lrb_sb", [8, 512], BF16)
                  lr8_sb = A2("lr8_sb", [8, 512], F32)
                  oneh_sb = A2("oneh_sb", [128, 136], BF16)
                  onehb_sb = A2("onehb_sb", [8, 1024], BF16)

                  pass

                  dma("sync", mask_sb[:, :], maskb_c, inc="CST")
                  dma("sync", oneh_sb[:, :], oneh_c, inc="CST")
                  cst2_n = dma("sync", onehb_sb[:, :], onehb_c, inc="CST")
                  wold = dma("gpsimd", wo_sb[:, :, :],
                             wo_h.rearrange("(h p) m -> p h m", p=128), inc="WOL")

                  qT_pool = Pool([0, 1])
                  pt_pool = Pool([0, 1, 2, 3])
                  otn_pool = Pool([0, 1])
                  x2_pool = Pool([0, 1, 2])
                  o1_pool = Pool([0, 1])
                  out1_cnt = [0, 0]
                  qt_ready = {}
                  kt_ready = {}
                  x2_ready = {}

                  def load_qT(qt):
                      slot, rel = qT_pool.acquire()
                      w = [("QRS", qr_mark[qt])]
                      n = None
                      for h in range(H):
                          n = dmat(qT_bufs[slot][:, h, :],
                                   qr_d[qt * 512:(qt + 1) * 512,
                                        h * 128:(h + 1) * 128],
                                   waits=(w + rel) if h == 0 else [],
                                   inc=f"QTL{slot}")
                      qt_ready[qt] = (slot, n)

                  def load_x2(blk):
                      slot, rel = x2_pool.acquire()
                      n = dma("sync", x2_bufs[slot][:, :],
                              x_h[blk * 128:(blk + 1) * 128, :],
                              waits=rel, inc=f"X2L{slot}")
                      x2_ready[blk] = (slot, n)

                  load_qT(0)
                  for half in range(2):
                      for h in range(H):
                          w = ([("KRS", kr_mark[3 if half == 0 else NT - 1])]
                               if h == 0 else [])
                          kt_ready[(h, half)] = dmat(
                              kT_sb[:, h, half * 2048:(half + 1) * 2048],
                              kr_d[half * 2048:(half + 1) * 2048,
                                   h * 128:(h + 1) * 128],
                              waits=w, inc=f"KTL{h}H{half}")
                  for blk in range(3):
                      load_x2(blk)

                  def emit_lo(st):
                      te_o = S.emit("tensor",
                                    lambda e, ob=st["ob"], sl=st["pt"], kc=st["kc"],
                                    h=st["h"], km=st["km"]: e.matmul(
                                        pb[ob][:, :],
                                        v_sb[:, kc, h * 128:(h + 1) * 128],
                                        pt_bufs[sl][:, :],
                                        start=(kc == 0), stop=(kc == km - 1)),
                                    waits=[("SE", st["se"]), ("CST", cst2_n)] +
                                          st["w0"],
                                    inc="TE" if st["kc"] == st["km"] - 1 else None)
                      te_l = S.emit("tensor",
                                    lambda e, lb=st["lb"], sl=st["pt"], kc=st["kc"],
                                    km=st["km"], h=st["h"]:
                                    e.matmul(pb[lb][0:128, :],
                                             oneh_sb[:, 7 - h:135 - h],
                                             pt_bufs[sl][:, :],
                                             start=(kc == 0 and h == 0),
                                             stop=(kc == km - 1 and h == H - 1)),
                                    inc="TE" if (st["kc"] == st["km"] - 1 and
                                                 st["h"] == H - 1) else None)
                      pt_pool.release(st["pt"], ("TE", S.cnt["TE"]))
                      return te_l, te_o

                  def emit_fin(fin, otn):
                      h, l_bank, o_bank, te_l, te_o = fin
                      se_o = S.emit("scalar", lambda e, ob=o_bank, h=h,
                                    otn=otn: e.copy(
                          otn[:, h, :], pb[ob][:, :]),
                          waits=[("TE", te_o if te_l is None
                                  else max(te_l, te_o))], inc="SE")
                      psum.release(o_bank, ("SE", se_o))
                      return se_o

                  def emit_norm(l_bank, otn, se_o_last):
                      se_n = S.emit("scalar", lambda e, lb=l_bank: e.activation(
                          lr8_sb[:, :], pb[lb][0:8, :], AF.Ln),
                          waits=[("SE", se_o_last)], inc="SE")
                      ve_n = S.emit("vector", lambda e: e.engine_nop(),
                                    waits=[("SE", se_n)], inc="VE")
                      psum.release(l_bank, ("VE", ve_n))
                      se_n = S.emit("scalar", lambda e: e.activation(
                          lrb_sb[:, :], lr8_sb[:, :], AF.Exp, scale=-1.0),
                          inc="SE")
                      otn_done = None
                      per_h = {}
                      for h in range(H):
                          b_bank, brel = psum.acquire()
                          te_b = S.emit("tensor", lambda e, bb=b_bank, h=h:
                                        e.matmul(pb[bb][:, :],
                                                 onehb_sb[:, h * 128:(h + 1) * 128],
                                                 lrb_sb[:, :],
                                                 start=True, stop=True),
                                        waits=[("SE", se_n)] + brel, inc="TE")
                          otn_done = S.emit("vector",
                                            lambda e, bb=b_bank, h=h, otn=otn:
                                            e.tensor_mul(otn[:, h, :],
                                                         otn[:, h, :],
                                                         pb[bb][:, :]),
                                            waits=[("TE", te_b)], inc="VE")
                          psum.release(b_bank, ("VE", otn_done))
                          per_h[h] = otn_done
                      return otn_done, per_h

                  for qt in range(NT):
                      if qt + 1 < NT:
                          load_qT(qt + 1)
                      qslot, qn = qt_ready[qt]
                      qT = qT_bufs[qslot]
                      oslot, orel = otn_pool.acquire()
                      otn = otn_bufs[oslot]
                      kc_max = 4 * qt + 4
                      otn_done = None
                      otn_ve = {}
                      pendq = deque()
                      pend_fin = None
                      l_bank, lrel = psum.acquire()
                      for h in range(H):
                          o_bank, orel2 = psum.acquire()
                          te_l = te_o = None
                          for kc in range(kc_max):
                              s_bank, srel = psum.acquire()
                              w = [(f"QTL{qslot}", qn),
                                   (f"KTL{h}H{kc // 16}",
                                    kt_ready[(h, kc // 16)])] + srel
                              if kc == 0 and h == 0:
                                  w += orel
                              te_s = S.emit("tensor",
                                            lambda e, sb=s_bank, h=h, kc=kc, qT=qT:
                                            e.matmul(
                                                pb[sb][:, :],
                                                kT_sb[:, h, kc * 128:(kc + 1) * 128],
                                                qT[:, h, :], start=True, stop=True),
                                            waits=w, inc="TE")
                              if len(pendq) >= 4:
                                  te_l, te_o = emit_lo(pendq.popleft())
                              if pend_fin is not None and (
                                      kc >= min(4, kc_max - 1)):
                                  emit_fin(pend_fin, otn)
                                  pend_fin = None
                              m = kc - 4 * qt
                              if m >= 0:
                                  ve_m = S.emit("vector", lambda e, sb=s_bank, m=m:
                                                e.tensor_add(
                                                    pb[sb][:, :], pb[sb][:, :],
                                                    mask_sb[:, 384 - 128 * m:
                                                            896 - 128 * m]),
                                                waits=[("TE", te_s),
                                                       ("CST", cst2_n)], inc="VE")
                                  expw = [("VE", ve_m)]
                              else:
                                  expw = [("TE", te_s)]
                              ptslot, ptrel = pt_pool.acquire()
                              se_e = S.emit("scalar", lambda e, sb=s_bank, sl=ptslot:
                                            e.activation(pt_bufs[sl][:, :],
                                                         pb[sb][:, :], AF.Exp),
                                            waits=expw + ptrel, inc="SE")
                              psum.release(s_bank, ("SE", se_e))
                              pendq.append({"lb": l_bank, "ob": o_bank,
                                            "pt": ptslot, "kc": kc, "km": kc_max,
                                            "h": h, "se": se_e,
                                            "w0": (lrel + orel2) if kc == 0
                                            else []})
                          while pendq:
                              te_l, te_o = emit_lo(pendq.popleft())
                          if pend_fin is not None:
                              emit_fin(pend_fin, otn)
                          pend_fin = (h, l_bank, o_bank, te_l, te_o)
                      se_o_last = emit_fin(pend_fin, otn)
                      pend_fin = None
                      otn_done, otn_ve = emit_norm(l_bank, otn, se_o_last)
                      qT_pool.release(qslot, ("TE", S.cnt["TE"]))

                      for tb in range(4):
                          blk = qt * 4 + tb
                          pair = []
                          relw = []
                          for half in range(2):
                              bank, rel = psum.acquire()
                              pair.append(bank)
                              relw += rel
                          te_p = None
                          for h in range(H):
                              for half in range(2):
                                  w = []
                                  if half == 0:
                                      w = [("VE", otn_ve[h])]
                                  if h == 0 and half == 0:
                                      w += [("WOL", wold)] + relw
                                  te_p = S.emit("tensor",
                                                lambda e, pk=pair[half], h=h,
                                                half=half, tb=tb, otn=otn: e.matmul(
                                                    pb[pk][:, :],
                                                    otn[:, h, tb * 128:(tb + 1) * 128],
                                                    wo_sb[:, h,
                                                          half * 512:(half + 1) * 512],
                                                    start=(h == 0), stop=(h == H - 1)),
                                                waits=w,
                                                inc="TE" if (h == H - 1 and half == 1)
                                                else None)
                          x2s, x2n = x2_ready[blk]
                          o1s, o1rel = o1_pool.acquire()
                          w = [("TE", te_p), (f"X2L{x2s}", x2n),
                               (f"OD1_{o1s}", out1_cnt[o1s])] + o1rel
                          S.emit("vector", lambda e, pk=pair[0], xs=x2s, os=o1s:
                                 e.tensor_add(o1_bufs[os][:, 0:512], pb[pk][:, :],
                                              x2_bufs[xs][:, 0:512]), waits=w)
                          ve_n = S.emit("vector",
                                        lambda e, pk=pair[1], xs=x2s, os=o1s:
                                        e.tensor_add(o1_bufs[os][:, 512:1024],
                                                     pb[pk][:, :],
                                                     x2_bufs[xs][:, 512:1024]),
                                        inc="VE")
                          x2_pool.release(x2s, ("VE", ve_n))
                          psum.release(pair[0], ("VE", ve_n))
                          psum.release(pair[1], ("VE", ve_n))
                          out1_cnt[o1s] = dma("gpsimd",
                                              out_h[blk * 128:(blk + 1) * 128, :],
                                              o1_bufs[o1s][:, :],
                                              waits=[("VE", ve_n)], inc=f"OD1_{o1s}")
                          if "d_o1" in dbg_h:
                              out1_cnt[o1s] = dma(
                                  "gpsimd", dbg_h["d_o1"][blk * 128:(blk + 1) * 128, :],
                                  o1_bufs[o1s][:, :], inc=f"OD1_{o1s}")
                          o1_pool.release(o1s, (f"OD1_{o1s}", out1_cnt[o1s]))
                          if blk + 3 < NB:
                              load_x2(blk + 3)
                      otn_pool.release(oslot, ("TE", S.cnt["TE"]))

                  if "d_v" in dbg_h:
                      for blk in range(NB):
                          dma("sync", dbg_h["d_v"][blk * 128:(blk + 1) * 128, :],
                              v_sb[:, blk, :], inc="DBG")

                  while conv_chunks:
                      emit_conv_chunk()

                  S.flush()

        with ExitStack() as s3:
          if 3 in DEBUG_PHASES:
              def A3(name, shape, dt):
                  return s3.enter_context(nc.sbuf_tensor(name, shape, dt))

              wgT_sb = A3("wgT_sb", [128, NDC, FF], BF16)
              wuT_sb = A3("wuT_sb", [128, NDC, FF], BF16)
              wdT_sb = A3("wdT_sb", [128, NHC, D], BF16)
              xnT3_bufs = [A3(f"xnT3_{i}", [128, NDC, 512], BF16) for i in range(2)]
              h_sb = A3("h_sb", [128, NHC, 512], BF16)
              g_bufs = [A3(f"g{i}", [128, 512], BF16) for i in range(2)]
              o13_bufs = [A3(f"o13_{i}", [128, D], F32) for i in range(3)]
              os_bufs = [A3(f"os{i}", [128, D], F32) for i in range(2)]

              conv_n = conv_total[0]
              xn_tot = [S.cnt.get("XNS0", 0), S.cnt.get("XNS1", 0)]
              od1_tot = [S.cnt.get("OD1_0", 0), S.cnt.get("OD1_1", 0)]

              xnT3_pool = Pool([0, 1])
              g_pool = Pool([0, 1])
              o13_pool = Pool([0, 1, 2])
              os_pool = Pool([0, 1])
              out_cnt = [0, 0]
              o13_ready = {}
              xnt3_ready = {}

              def load_xnT3(t):
                  slot, rel = xnT3_pool.acquire()
                  w = ([("XNS0", xn_tot[0]), ("XNS1", xn_tot[1])] if t == 0 else [])
                  n = None
                  for dc in range(NDC):
                      n = dmat(xnT3_bufs[slot][:, dc, :],
                               xn_d[t * 512:(t + 1) * 512, dc * 128:(dc + 1) * 128],
                               waits=(w + rel) if dc == 0 else [],
                               inc=f"XT3{slot}")
                  xnt3_ready[t] = (slot, n)

              def load_o13(blk):
                  slot, rel = o13_pool.acquire()
                  w = ([("OD1_0", od1_tot[0]), ("OD1_1", od1_tot[1])]
                       if blk == 0 else [])
                  n = dma("sync", o13_bufs[slot][:, :],
                          out_h[blk * 128:(blk + 1) * 128, :],
                          waits=w + rel, inc=f"O13{slot}")
                  o13_ready[blk] = (slot, n)

              wg_n = wu_n = wd_n = None
              for dc in range(NDC):
                  w = [("CONV", conv_n)] if dc == 0 else []
                  wg_n = dmat(wgT_sb[:, dc, :], wg_b[:, dc * 128:(dc + 1) * 128],
                              waits=w, inc="WGT")
              load_xnT3(0)
              for dc in range(NDC):
                  wu_n = dmat(wuT_sb[:, dc, :], wu_b[:, dc * 128:(dc + 1) * 128],
                              inc="WUT")
              for blk in range(3):
                  load_o13(blk)
              for hc in range(NHC):
                  wd_n = dmat(wdT_sb[:, hc, :], wd_b[:, hc * 128:(hc + 1) * 128],
                              inc="WDT")

              for t in range(NT):
                  slot, xnt_n = xnt3_ready[t]
                  xnT = xnT3_bufs[slot]

                  hc_done = None
                  for hc in range(NHC):
                      gb, grel = psum.acquire()
                      ub, urel = psum.acquire()
                      te_g = te_u = None
                      for dc in range(NDC):
                          w = []
                          if dc == 0:
                              w = [(f"XT3{slot}", xnt_n),
                                   ("WGT", 16)] + grel
                          if t == 0:
                              w += [("WGT", 16 * (dc + 1))]
                          te_g = S.emit("tensor",
                                        lambda e, gb=gb, dc=dc, hc=hc, xnT=xnT:
                                        e.matmul(
                                            pb[gb][:, :],
                                            wgT_sb[:, dc, hc * 128:(hc + 1) * 128],
                                            xnT[:, dc, :],
                                            start=(dc == 0), stop=(dc == NDC - 1)),
                                        waits=w, inc="TE" if dc == NDC - 1 else None)
                      for dc in range(NDC):
                          w = []
                          if dc == 0:
                              w = [("WUT", 16)] + urel
                          if t == 0:
                              w += [("WUT", 16 * (dc + 1))]
                          te_u = S.emit("tensor",
                                        lambda e, ub=ub, dc=dc, hc=hc, xnT=xnT:
                                        e.matmul(
                                            pb[ub][:, :],
                                            wuT_sb[:, dc, hc * 128:(hc + 1) * 128],
                                            xnT[:, dc, :],
                                            start=(dc == 0), stop=(dc == NDC - 1)),
                                        waits=w, inc="TE" if dc == NDC - 1 else None)
                      gs, gsrel = g_pool.acquire()
                      se_n = S.emit("scalar", lambda e, gb=gb, gs=gs: e.activation(
                          g_bufs[gs][:, :], pb[gb][:, :], AF.Silu, scale=INV127),
                          waits=[("TE", te_g)] + gsrel, inc="SE")
                      psum.release(gb, ("SE", se_n))
                      hc_done = S.emit("vector", lambda e, ub=ub, gs=gs, hc=hc:
                                       e.scalar_tensor_tensor(
                                           h_sb[:, hc, :], pb[ub][:, :], INV127,
                                           g_bufs[gs][:, :], ALU.mult, ALU.mult),
                                       waits=[("TE", te_u), ("SE", se_n)], inc="VE")
                      psum.release(ub, ("VE", hc_done))
                      g_pool.release(gs, ("VE", hc_done))

                  xnT3_pool.release(slot, ("TE", S.cnt["TE"]))
                  if t + 1 < NT:
                      load_xnT3(t + 1)

                  for tb in range(4):
                      blk = 4 * t + tb
                      pair = []
                      relw = []
                      for half in range(2):
                          bank, rel = psum.acquire()
                          pair.append(bank)
                          relw += rel
                      te_d = None
                      for hc in range(NHC):
                          for half in range(2):
                              w = []
                              if hc == 0 and half == 0:
                                  w = [("VE", hc_done), ("WDT", 16)] + relw
                              if t == 0 and half == 0:
                                  w += [("WDT", 16 * (hc + 1))]
                              te_d = S.emit("tensor",
                                            lambda e, pk=pair[half], hc=hc, tb=tb,
                                            half=half: e.matmul(
                                                pb[pk][:, :],
                                                h_sb[:, hc, tb * 128:(tb + 1) * 128],
                                                wdT_sb[:, hc,
                                                       half * 512:(half + 1) * 512],
                                                start=(hc == 0), stop=(hc == NHC - 1)),
                                            waits=w,
                                            inc="TE" if (hc == NHC - 1 and half == 1)
                                            else None)
                      o13s, o13n = o13_ready[blk]
                      oss, osrel = os_pool.acquire()
                      w = [("TE", te_d), (f"O13{o13s}", o13n),
                           (f"ODF{oss}", out_cnt[oss])] + osrel
                      S.emit("vector", lambda e, pk=pair[0], o13s=o13s, oss=oss:
                             e.scalar_tensor_tensor(
                                 os_bufs[oss][:, 0:512], pb[pk][:, :], INV127,
                                 o13_bufs[o13s][:, 0:512], ALU.mult, ALU.add),
                             waits=w)
                      ve_n = S.emit("vector",
                                    lambda e, pk=pair[1], o13s=o13s, oss=oss:
                                    e.scalar_tensor_tensor(
                                        os_bufs[oss][:, 512:1024], pb[pk][:, :],
                                        INV127, o13_bufs[o13s][:, 512:1024],
                                        ALU.mult, ALU.add),
                                    inc="VE")
                      o13_pool.release(o13s, ("VE", ve_n))
                      psum.release(pair[0], ("VE", ve_n))
                      psum.release(pair[1], ("VE", ve_n))
                      out_cnt[oss] = dma("gpsimd", out_h[blk * 128:(blk + 1) * 128, :],
                                         os_bufs[oss][:, :], waits=[("VE", ve_n)],
                                         inc=f"ODF{oss}")
                      os_pool.release(oss, (f"ODF{oss}", out_cnt[oss]))
                      if blk + 3 < NB:
                          load_o13(blk + 3)

              for nm, srcbuf in (("d_xn", xn_d), ("d_qr", qr_d), ("d_kr", kr_d)):
                  if nm in dbg_h:
                      dma("gpsimd", dbg_h[nm], srcbuf, inc="DBG")

              S.emit("gpsimd", lambda e: e.engine_nop(),
                     waits=[("ODF0", out_cnt[0]), ("ODF1", out_cnt[1])])

              S.flush()

    return nc


def kernel(L0_attn_x, L0_attn_wq, L0_attn_wk, L0_attn_wv, L0_attn_wo,
           L0_ffn_w_gate, L0_ffn_w_up, L0_ffn_w_down):
    nc = build_program()
    wq = np.ascontiguousarray(np.asarray(L0_attn_wq).reshape(D, D))
    wk = np.ascontiguousarray(np.asarray(L0_attn_wk).reshape(D, D))
    wv = np.ascontiguousarray(np.asarray(L0_attn_wv).reshape(D, D))
    wo = np.ascontiguousarray(np.asarray(L0_attn_wo).reshape(D, D))
    wg = np.ascontiguousarray(np.asarray(L0_ffn_w_gate))
    wu = np.ascontiguousarray(np.asarray(L0_ffn_w_up))
    wd = np.ascontiguousarray(np.asarray(L0_ffn_w_down))
    x = np.asarray(L0_attn_x)
    in_maps = [{"x": np.ascontiguousarray(x[b]), "wq": wq, "wk": wk, "wv": wv,
                "wo": wo, "wg": wg, "wu": wu, "wd": wd} for b in range(8)]
    res = run_bass_kernel_spmd(nc, in_maps, core_ids=list(range(8)))
    return np.stack([res.results[b]["out"] for b in range(8)], axis=0)
